# version 04
# baseline (speedup 1.0000x reference)
import math
import contextlib
import numpy as np
import ml_dtypes
import concourse.bass as bass
import concourse.mybir as mybir
from concourse.bass_utils import run_bass_kernel_spmd

F32 = mybir.dt.float32
BF16 = mybir.dt.bfloat16
FP8 = mybir.dt.float8e4
AF = mybir.ActivationFunctionType
ALU = mybir.AluOpType
AX = mybir.AxisListType

D = 4096
TOK = 2048
NCH = 16
NJB = D // 128
NSPAN = D // 512
I_DIM = 11008
NIB = I_DIM // 128
TB = 512
NBLK = TOK // TB
NTSUB = TB // 128
KPLEN = 128 + 4608
TXW = 4608
N_STATE = 64
EPS_BL = 1e-5
EPS_LN = 1e-5
WG_N = I_DIM * D


def _host_tables():
    L = D
    l = np.arange(L)
    Om = np.exp(-2j * np.pi * l / L)
    T1 = (1.0 - Om) / (1.0 + Om)
    T2 = 2.0 / (1.0 + Om)
    t = {}
    t["t1r"] = T1.real.astype(np.float32)[None, :]
    t["t1i"] = T1.imag.astype(np.float32)[None, :]
    t["t2r"] = T2.real.astype(np.float32)[None, :]
    t["t2i"] = T2.imag.astype(np.float32)[None, :]
    l1 = np.arange(64)[:, None]
    n0 = np.arange(64)[None, :]
    cA = np.cos(2 * np.pi * n0 * l1 / 64).astype(np.float32)
    sA = np.sin(2 * np.pi * n0 * l1 / 64).astype(np.float32)
    t["wa"] = np.concatenate([cA, sA], axis=1)
    t["wa2"] = np.concatenate([-sA, cA], axis=1)
    l0 = np.arange(64)[:, None]
    t["twr"] = (np.cos(2 * np.pi * n0 * l0 / 4096) / 4096.0).astype(np.float32)
    t["twi"] = (np.sin(2 * np.pi * n0 * l0 / 4096) / 4096.0).astype(np.float32)
    n1 = np.arange(64)[None, :]
    t["wbc"] = np.cos(2 * np.pi * n1 * l0 / 64).astype(np.float32)
    t["wbs"] = (-np.sin(2 * np.pi * n1 * l0 / 64)).astype(np.float32)
    t["ident"] = np.eye(128, dtype=np.float32).astype(ml_dtypes.bfloat16)
    t["ones1x64"] = np.ones((1, 64), np.float32)
    t["ones1x128"] = np.ones((1, 128), np.float32)
    t["ones64x1"] = np.ones((64, 1), np.float32)
    t["ones128x1"] = np.ones((128, 1), np.float32)
    return t


def build_program():
    nc = bass.Bass("TRN2", target_bir_lowering=False)

    u_d = nc.dram_tensor("u", [TOK, D], F32, kind="ExternalInput").ap()
    wgT = nc.dram_tensor("wgT", [D, I_DIM], F32, kind="ExternalInput").ap()
    wuT = nc.dram_tensor("wuT", [D, I_DIM], F32, kind="ExternalInput").ap()
    wdT = nc.dram_tensor("wdT", [I_DIM, D], F32, kind="ExternalInput").ap()
    scol = nc.dram_tensor("scol", [64, 8], F32, kind="ExternalInput").ap()
    vcr_d = nc.dram_tensor("vcr", [64, 64], F32, kind="ExternalInput").ap()
    vci_d = nc.dram_tensor("vci", [64, 64], F32, kind="ExternalInput").ap()
    bprow_d = nc.dram_tensor("bprow", [1, 64], F32, kind="ExternalInput").ap()
    dls_d = nc.dram_tensor("dls", [1, 2], F32, kind="ExternalInput").ap()
    cvals_d = nc.dram_tensor("cvals", [1, 4], F32, kind="ExternalInput").ap()
    tabs = {}
    for nm, shp in [("t1r", [1, D]), ("t1i", [1, D]), ("t2r", [1, D]), ("t2i", [1, D]),
                    ("wa", [64, 128]), ("wa2", [64, 128]), ("twr", [64, 64]),
                    ("twi", [64, 64]), ("wbc", [64, 64]), ("wbs", [64, 64]),
                    ("ones1x64", [1, 64]), ("ones1x128", [1, 128]),
                    ("ones64x1", [64, 1]), ("ones128x1", [128, 1])]:
        tabs[nm] = nc.dram_tensor(nm, shp, F32, kind="ExternalInput").ap()
    ident_d = nc.dram_tensor("ident", [128, 128], BF16, kind="ExternalInput").ap()
    out_d = nc.dram_tensor("out", [TOK, D], F32, kind="ExternalOutput").ap()

    kp_d = nc.dram_tensor("kp", [KPLEN], BF16, kind="Internal")
    bounce_d = nc.dram_tensor("bounce", [16, 4096], F32, kind="Internal").ap()
    wgq_d = nc.dram_tensor("wgq", [NIB, 32, 128, 128], FP8, kind="Internal").ap()
    wuq_d = nc.dram_tensor("wuq", [NIB, 32, 128, 128], FP8, kind="Internal").ap()
    wdq_d = nc.dram_tensor("wdq", [4, NIB, 128, 1024], FP8, kind="Internal").ap()
    l3t_d = nc.dram_tensor("l3t", [NBLK, NJB, 128, TB], BF16, kind="Internal").ap()
    x2_d = nc.dram_tensor("x2sp", [NCH, 128, D], F32, kind="Internal").ap()
    dsb_d = nc.dram_tensor("dsb", [NBLK, NTSUB, 128, D], F32, kind="Internal").ap()

    st = contextlib.ExitStack()
    sems = {}
    for nm in ["d_small", "d_u", "d_p1", "d_tx", "d_sp", "d_l3", "d_w", "d_wd",
               "d_ap", "d_out", "d_wb", "d_bn", "d_wr", "t_s4", "t_tr", "t_cv", "t_g",
               "t_d", "t_di", "t_bc", "v_s4", "v_mx", "s_mx", "v_ff", "s_ff", "v_q",
               "s_q", "v_st", "s_ev", "v_ap"]:
        sems[nm] = st.enter_context(nc.semaphore(nm))
    cnt = {nm: 0 for nm in sems}

    ops = {"sync": [], "tensor": [], "vector": [], "scalar": [], "gpsimd": []}

    def em(e, fn):
        ops[e].append(fn)

    def inc_of(instr, pairs):
        for s, n in pairs:
            instr = instr.then_inc(sems[s], n)
        return instr

    def dma(e, dst, src, s, **kw):
        em(e, lambda eng, s=s, v=cnt[s]: eng.wait_ge(sems[s], v))
        cnt[s] += 16
        em(e, lambda eng, dst=dst, src=src, kw=kw, s=s:
           eng.dma_start(out=dst, in_=src, **kw).then_inc(sems[s], 16))

    def wait(e, s, val=None):
        v = cnt[s] if val is None else val
        em(e, lambda eng, s=s, v=v: eng.wait_ge(sems[s], v))

    def op(e, f, incs=()):
        for s, n in incs:
            cnt[s] += n
        em(e, lambda eng, f=f, incs=tuple(incs): inc_of(f(eng), incs))

    glob = contextlib.ExitStack()
    tallx = glob.enter_context(nc.sbuf_tensor("tallx", [128, TXW], BF16))
    ident = glob.enter_context(nc.sbuf_tensor("ident_s", [128, 128], BF16))
    ones128 = glob.enter_context(nc.sbuf_tensor("ones128", [1, 128], F32))
    ones64c = glob.enter_context(nc.sbuf_tensor("ones64c", [64, 1], F32))
    ones128c = glob.enter_context(nc.sbuf_tensor("ones128c", [128, 1], F32))
    cscal = glob.enter_context(nc.sbuf_tensor("cscal", [128, 12], F32))
    p1slot = glob.enter_context(nc.sbuf_tensor("p1slot", [128, 224], F32))
    svrow = glob.enter_context(nc.sbuf_tensor("svrow", [1, 16], F32))
    cvals = glob.enter_context(nc.sbuf_tensor("cvals_s", [1, 4], F32))

    dma("sync", ident[:], ident_d, "d_small")
    dma("sync", ones128[:], tabs["ones1x128"], "d_small")
    dma("sync", ones64c[:], tabs["ones64x1"], "d_small")
    dma("sync", ones128c[:], tabs["ones128x1"], "d_small")
    dma("sync", cvals[:], cvals_d, "d_small")

    s4 = contextlib.ExitStack()
    b = [s4.enter_context(nc.sbuf_tensor(f"s4b{i}", [64, D], F32)) for i in range(6)]
    vcr = s4.enter_context(nc.sbuf_tensor("vcr_s", [64, 64], F32))
    vci = s4.enter_context(nc.sbuf_tensor("vci_s", [64, 64], F32))
    bpr = s4.enter_context(nc.sbuf_tensor("bpr_s", [1, 64], F32))
    scs = s4.enter_context(nc.sbuf_tensor("scs", [64, 8], F32))
    dls = s4.enter_context(nc.sbuf_tensor("dls_s", [1, 2], F32))
    numc = s4.enter_context(nc.sbuf_tensor("numc", [64, 10], F32))
    t2rows = s4.enter_context(nc.sbuf_tensor("t2rows", [1, 2 * D], F32))
    krow_st = s4.enter_context(nc.sbuf_tensor("krow_st", [1, 512], F32))
    kc = s4.enter_context(nc.sbuf_tensor("kc", [64, 2 * 64], F32))
    karg = s4.enter_context(nc.sbuf_tensor("karg", [64, 4 * 64], F32))
    kmod = s4.enter_context(nc.sbuf_tensor("kmod", [64, 64], F32))
    kmodb = s4.enter_context(nc.sbuf_tensor("kmodb", [64, 64], BF16))
    kmodbT = s4.enter_context(nc.sbuf_tensor("kmodbT", [64, 64], BF16))
    zrow = s4.enter_context(nc.sbuf_tensor("zrow", [1, 640], BF16))
    was = s4.enter_context(nc.sbuf_tensor("was", [64, 128], F32))
    wa2s = s4.enter_context(nc.sbuf_tensor("wa2s", [64, 128], F32))
    twrs = s4.enter_context(nc.sbuf_tensor("twrs", [64, 64], F32))
    twis = s4.enter_context(nc.sbuf_tensor("twis", [64, 64], F32))
    wbcs = s4.enter_context(nc.sbuf_tensor("wbcs", [64, 64], F32))
    wbss = s4.enter_context(nc.sbuf_tensor("wbss", [64, 64], F32))
    ones64r = s4.enter_context(nc.sbuf_tensor("ones64r", [1, 64], F32))

    ps4 = contextlib.ExitStack()
    p_rep = ps4.enter_context(nc.psum_tensor("p_rep", [64, 512], F32))
    p_row = ps4.enter_context(nc.psum_tensor("p_row", [1, 512], F32))
    p_A = ps4.enter_context(nc.psum_tensor("p_A", [64, 128], F32))
    p_K = ps4.enter_context(nc.psum_tensor("p_K", [64, 64], F32))
    p_Kt = ps4.enter_context(nc.psum_tensor("p_Kt", [64, 64], BF16))
    p_bc = ps4.enter_context(nc.psum_tensor("p_bc", [128, 12], F32))

    for nm, dst in [("wa", was), ("wa2", wa2s), ("twr", twrs), ("twi", twis),
                    ("wbc", wbcs), ("wbs", wbss), ("ones1x64", ones64r)]:
        dma("sync", dst[:], tabs[nm], "d_small")
    dma("sync", vcr[:], vcr_d, "d_small")
    dma("sync", vci[:], vci_d, "d_small")
    dma("sync", bpr[:], bprow_d, "d_small")
    dma("sync", scs[:], scol, "d_small")
    dma("sync", dls[:], dls_d, "d_small")
    dma("sync", t2rows[:, 0:D], tabs["t2r"], "d_small")
    dma("sync", t2rows[:, D:2 * D], tabs["t2i"], "d_small")
    t1rows = s4.enter_context(nc.sbuf_tensor("t1rows", [1, 2 * D], F32))
    dma("sync", t1rows[:, 0:D], tabs["t1r"], "d_small")
    dma("sync", t1rows[:, D:2 * D], tabs["t1i"], "d_small")

    wait("tensor", "d_small")
    wait("vector", "d_small")
    wait("scalar", "d_small")

    op("scalar", lambda e: e.activation(svrow[:, 0:1], dls[:, 1:2], AF.Exp,
                                        scale=cvals[:, 0:1], bias=cvals[:, 1:2]), [("s_q", 1)])
    wait("tensor", "s_q")
    op("tensor", lambda e: e.matmul(p_bc[0:64, 0:1], ones64r[:], svrow[:, 0:1],
                                    start=True, stop=True), [("t_bc", 1)])
    cgcol = numc[:, 0:1]
    cgc = s4.enter_context(nc.sbuf_tensor("cgc", [64, 1], F32))
    wait("vector", "t_bc")
    op("vector", lambda e: e.tensor_copy(cgc[:], p_bc[0:64, 0:1]), [("v_s4", 1)])

    op("tensor", lambda e: e.matmul(p_rep[:, 0:64], ones64r[:], bpr[:],
                                    start=True, stop=True), [("t_bc", 1)])
    wait("vector", "t_bc")
    op("vector", lambda e: e.tensor_mul(kc[:, 0:64], vcr[:], p_rep[:, 0:64]), [("v_s4", 1)])
    op("vector", lambda e: e.tensor_mul(kc[:, 64:128], vci[:], p_rep[:, 0:64]), [("v_s4", 1)])
    wait("vector", "v_s4")
    op("vector", lambda e: e.reduce_sum(numc[:, 0:1], kc[:, 0:64], AX.X), [("v_s4", 1)])
    op("vector", lambda e: e.reduce_sum(numc[:, 1:2], kc[:, 64:128], AX.X), [("v_s4", 1)])
    wait("vector", "v_s4")
    def cplx_conj_mul(dst_r, dst_i, xr, xi, yr, yi):
        op("vector", lambda e: e.tensor_mul(kc[:, 0:1], xr, yr), [("v_s4", 1)])
        op("vector", lambda e: e.tensor_mul(kc[:, 1:2], xi, yi), [("v_s4", 1)])
        op("vector", lambda e: e.tensor_mul(kc[:, 2:3], xr, yi), [("v_s4", 1)])
        op("vector", lambda e: e.tensor_mul(kc[:, 3:4], xi, yr), [("v_s4", 1)])
        wait("vector", "v_s4")
        op("vector", lambda e: e.tensor_add(dst_r, kc[:, 0:1], kc[:, 1:2]), [("v_s4", 1)])
        op("vector", lambda e: e.tensor_sub(dst_i, kc[:, 2:3], kc[:, 3:4]), [("v_s4", 1)])
        wait("vector", "v_s4")

    cplx_conj_mul(numc[:, 2:3], numc[:, 3:4], scs[:, 6:7], scs[:, 7:8], numc[:, 0:1], numc[:, 1:2])
    cplx_conj_mul(numc[:, 4:5], numc[:, 5:6], scs[:, 6:7], scs[:, 7:8], scs[:, 2:3], scs[:, 3:4])
    cplx_conj_mul(numc[:, 6:7], numc[:, 7:8], scs[:, 4:5], scs[:, 5:6], numc[:, 0:1], numc[:, 1:2])
    cplx_conj_mul(numc[:, 8:9], numc[:, 9:10], scs[:, 4:5], scs[:, 5:6], scs[:, 2:3], scs[:, 3:4])
    nnum = s4.enter_context(nc.sbuf_tensor("nnum", [64, 4], F32))
    for j in range(4):
        op("vector", lambda e, j=j: e.tensor_scalar_mul(nnum[:, j:j + 1],
                                                        numc[:, 2 + 2 * j:3 + 2 * j], -1.0), [("v_s4", 1)])
    wait("vector", "v_s4")

    wait("tensor", "v_s4")
    nbc = 0
    for comp, buf in ((0, b[0]), (1, b[1])):
        for ch in range(8):
            op("tensor", lambda e, comp=comp, ch=ch: e.matmul(
                p_rep[:], ones64r[:], t1rows[:, comp * D + ch * 512: comp * D + (ch + 1) * 512],
                start=True, stop=True), [("t_bc", 1)])
            nbc += 1
            wait("vector", "t_bc")
            op("vector", lambda e, buf=buf, ch=ch: e.tensor_copy(
                buf[:, ch * 512:(ch + 1) * 512], p_rep[:]), [("v_s4", 1)])
            wait("tensor", "v_s4")

    wait("vector", "v_s4")
    op("vector", lambda e: e.tensor_scalar(b[2][:], b[0][:], cgc[:], scs[:, 0:1],
                                           ALU.mult, ALU.subtract), [("v_s4", 1)])
    op("vector", lambda e: e.tensor_scalar(b[3][:], b[1][:], cgc[:], scs[:, 1:2],
                                           ALU.mult, ALU.subtract), [("v_s4", 1)])
    wait("vector", "v_s4")
    op("vector", lambda e: e.tensor_mul(b[4][:], b[2][:], b[2][:]), [("v_s4", 1)])
    op("vector", lambda e: e.tensor_mul(b[5][:], b[3][:], b[3][:]), [("v_s4", 1)])
    wait("vector", "v_s4")
    op("vector", lambda e: e.tensor_add(b[4][:], b[4][:], b[5][:]), [("v_s4", 1)])
    wait("vector", "v_s4")
    op("vector", lambda e: e.reciprocal(b[5][:], b[4][:]), [("v_s4", 1)])
    wait("vector", "v_s4")
    op("vector", lambda e: e.tensor_mul(b[0][:], b[2][:], b[5][:]), [("v_s4", 1)])
    op("vector", lambda e: e.tensor_mul(b[1][:], b[3][:], b[5][:]), [("v_s4", 1)])
    wait("vector", "v_s4")

    for j in range(4):
        nr = numc[:, 2 + 2 * j:3 + 2 * j]
        ni = numc[:, 3 + 2 * j:4 + 2 * j]
        nnr = nnum[:, j:j + 1]
        op("vector", lambda e, nr=nr: e.tensor_scalar_mul(b[2][:], b[0][:], nr), [("v_s4", 1)])
        op("vector", lambda e, ni=ni: e.tensor_scalar_mul(b[3][:], b[0][:], ni), [("v_s4", 1)])
        wait("vector", "v_s4")
        op("vector", lambda e, ni=ni: e.scalar_tensor_tensor(b[2][:], b[1][:], ni, b[2][:],
                                                             ALU.mult, ALU.add), [("v_s4", 1)])
        op("vector", lambda e, nnr=nnr: e.scalar_tensor_tensor(b[3][:], b[1][:], nnr, b[3][:],
                                                               ALU.mult, ALU.add), [("v_s4", 1)])
        wait("vector", "v_s4")
        for comp, buf in ((2 * j, b[2]), (2 * j + 1, b[3])):
            for ch in range(8):
                wait("tensor", "v_s4")
                op("tensor", lambda e, buf=buf, ch=ch: e.matmul(
                    p_row[:], ones64c[:], buf[:, ch * 512:(ch + 1) * 512],
                    start=True, stop=True), [("t_bc", 1)])
                wait("vector", "t_bc")
                op("vector", lambda e, ch=ch: e.tensor_copy(krow_st[:], p_row[:]), [("v_s4", 1)])
                wait("sync", "v_s4")
                dma("sync", bounce_d[comp, ch * 512:(ch + 1) * 512], krow_st[:], "d_bn")
                wait("vector", "d_bn")
                wait("tensor", "d_bn")

    k8 = s4.enter_context(nc.sbuf_tensor("k8", [64, 8 * 64], F32))
    wait("sync", "d_bn")
    for comp in range(8):
        dma("sync", k8[:, comp * 64:(comp + 1) * 64],
            bounce_d[comp, :].rearrange("(a b) -> a b", a=64), "d_bn")
    wait("sync", "d_bn")
    wait("vector", "d_bn")
    KR = lambda c: k8[:, c * 64:(c + 1) * 64]
    tA = s4.enter_context(nc.sbuf_tensor("tA", [64, 6 * 64], F32))
    TT = lambda c: tA[:, c * 64:(c + 1) * 64]
    op("vector", lambda e: e.tensor_mul(TT(0), KR(2), KR(4)), [("v_s4", 1)])
    op("vector", lambda e: e.tensor_mul(TT(1), KR(3), KR(5)), [("v_s4", 1)])
    op("vector", lambda e: e.tensor_mul(TT(2), KR(2), KR(5)), [("v_s4", 1)])
    op("vector", lambda e: e.tensor_mul(TT(3), KR(3), KR(4)), [("v_s4", 1)])
    wait("vector", "v_s4")
    op("vector", lambda e: e.tensor_sub(TT(0), TT(0), TT(1)), [("v_s4", 1)])
    wait("vector", "v_s4")
    op("vector", lambda e: e.tensor_add(TT(1), TT(2), TT(3)), [("v_s4", 1)])
    wait("vector", "v_s4")
    op("vector", lambda e: e.tensor_scalar_add(TT(2), KR(6), 1.0), [("v_s4", 1)])
    wait("vector", "v_s4")
    op("vector", lambda e: e.tensor_mul(TT(4), TT(2), TT(2)), [("v_s4", 1)])
    op("vector", lambda e: e.tensor_mul(TT(5), KR(7), KR(7)), [("v_s4", 1)])
    wait("vector", "v_s4")
    op("vector", lambda e: e.tensor_add(TT(4), TT(4), TT(5)), [("v_s4", 1)])
    wait("vector", "v_s4")
    op("vector", lambda e: e.reciprocal(TT(5), TT(4)), [("v_s4", 1)])
    wait("vector", "v_s4")
    op("vector", lambda e: e.tensor_mul(TT(4), TT(0), TT(2)), [("v_s4", 1)])
    wait("vector", "v_s4")
    op("vector", lambda e: e.tensor_mul(kc[:, 0:64], TT(1), KR(7)), [("v_s4", 1)])
    wait("vector", "v_s4")
    op("vector", lambda e: e.tensor_add(TT(4), TT(4), kc[:, 0:64]), [("v_s4", 1)])
    wait("vector", "v_s4")
    op("vector", lambda e: e.tensor_mul(TT(4), TT(4), TT(5)), [("v_s4", 1)])
    op("vector", lambda e: e.tensor_mul(kc[:, 0:64], TT(1), TT(2)), [("v_s4", 1)])
    op("vector", lambda e: e.tensor_mul(kc[:, 64:128], TT(0), KR(7)), [("v_s4", 1)])
    wait("vector", "v_s4")
    op("vector", lambda e: e.tensor_sub(kc[:, 0:64], kc[:, 0:64], kc[:, 64:128]), [("v_s4", 1)])
    wait("vector", "v_s4")
    op("vector", lambda e: e.tensor_mul(TT(5), kc[:, 0:64], TT(5)), [("v_s4", 1)])
    wait("vector", "v_s4")
    op("vector", lambda e: e.tensor_sub(TT(0), KR(0), TT(4)), [("v_s4", 1)])
    op("vector", lambda e: e.tensor_sub(TT(1), KR(1), TT(5)), [("v_s4", 1)])
    wait("vector", "v_s4")
    t2t = s4.enter_context(nc.sbuf_tensor("t2t", [64, 128], F32))
    dma("sync", bounce_d[8, :], t2rows[:, 0:D], "d_bn")
    dma("sync", bounce_d[9, :], t2rows[:, D:2 * D], "d_bn")
    wait("sync", "d_bn")
    dma("sync", t2t[:, 0:64], bounce_d[8, :].rearrange("(a b) -> a b", a=64), "d_bn")
    dma("sync", t2t[:, 64:128], bounce_d[9, :].rearrange("(a b) -> a b", a=64), "d_bn")
    wait("vector", "d_bn")
    op("vector", lambda e: e.tensor_mul(TT(2), t2t[:, 0:64], TT(0)), [("v_s4", 1)])
    op("vector", lambda e: e.tensor_mul(TT(3), t2t[:, 64:128], TT(1)), [("v_s4", 1)])
    op("vector", lambda e: e.tensor_mul(TT(4), t2t[:, 0:64], TT(1)), [("v_s4", 1)])
    op("vector", lambda e: e.tensor_mul(TT(5), t2t[:, 64:128], TT(0)), [("v_s4", 1)])
    wait("vector", "v_s4")
    op("vector", lambda e: e.tensor_sub(karg[:, 0:64], TT(2), TT(3)), [("v_s4", 1)])
    op("vector", lambda e: e.tensor_add(karg[:, 64:128], TT(4), TT(5)), [("v_s4", 1)])
    wait("vector", "v_s4")

    wait("tensor", "v_s4")
    op("tensor", lambda e: e.matmul(p_A[:], karg[:, 0:64], was[:], start=True, stop=False))
    op("tensor", lambda e: e.matmul(p_A[:], karg[:, 64:128], wa2s[:], start=False, stop=True),
       [("t_s4", 1)])
    wait("vector", "t_s4")
    op("vector", lambda e: e.tensor_copy(karg[:, 128:256], p_A[:]), [("v_s4", 1)])
    wait("vector", "v_s4")
    Ar = karg[:, 128:192]
    Ai = karg[:, 192:256]
    op("vector", lambda e: e.tensor_mul(TT(0), Ar, twrs[:]), [("v_s4", 1)])
    op("vector", lambda e: e.tensor_mul(TT(1), Ai, twis[:]), [("v_s4", 1)])
    op("vector", lambda e: e.tensor_mul(TT(2), Ar, twis[:]), [("v_s4", 1)])
    op("vector", lambda e: e.tensor_mul(TT(3), Ai, twrs[:]), [("v_s4", 1)])
    wait("vector", "v_s4")
    op("vector", lambda e: e.tensor_sub(TT(0), TT(0), TT(1)), [("v_s4", 1)])
    wait("vector", "v_s4")
    op("vector", lambda e: e.tensor_add(TT(1), TT(2), TT(3)), [("v_s4", 1)])
    wait("vector", "v_s4")
    wait("tensor", "v_s4")
    op("tensor", lambda e: e.matmul(p_K[:], TT(0), wbcs[:], start=True, stop=False))
    op("tensor", lambda e: e.matmul(p_K[:], TT(1), wbss[:], start=False, stop=True),
       [("t_s4", 1)])
    wait("vector", "t_s4")
    op("vector", lambda e: e.tensor_copy(kmod[:], p_K[:]), [("v_s4", 1)])
    wait("vector", "v_s4")
    op("vector", lambda e: e.tensor_add(kmod[0:1, 0:1], kmod[0:1, 0:1], dls[:, 0:1]), [("v_s4", 1)])
    wait("vector", "v_s4")
    op("vector", lambda e: e.tensor_copy(kmodb[:], kmod[:]), [("v_s4", 1)])
    wait("tensor", "v_s4")
    op("tensor", lambda e: e.transpose(p_Kt[:], kmodb[:], ident[0:64, 0:64]), [("t_s4", 1)])
    wait("vector", "t_s4")
    op("vector", lambda e: e.tensor_copy(kmodbT[:], p_Kt[:]), [("v_s4", 1)])
    op("vector", lambda e: e.memset(zrow[:], 0.0), [("v_s4", 1)])
    wait("sync", "v_s4")
    dma("sync", bass.AP(kp_d, 0, [[1, 1], [1, 640]]), zrow[:], "d_tx")
    dma("sync", bass.AP(kp_d, 640, [[64, 64], [1, 64]]), kmodbT[:], "d_tx")
    wait("sync", "d_tx")
    wait("scalar", "d_tx")
    for p in range(128):
        q = "sync" if p % 2 == 0 else "scalar"
        dma(q, tallx[p:p + 1, :], bass.AP(kp_d, 128 - p, [[1, 1], [1, TXW]]), "d_tx")

    ps4.close()
    s4.close()

    mx = contextlib.ExitStack()
    ub = [mx.enter_context(nc.sbuf_tensor(f"ub{i}", [128, D], F32)) for i in range(2)]
    ysb = mx.enter_context(nc.sbuf_tensor("ysb", [128, D], F32))
    x2b = mx.enter_context(nc.sbuf_tensor("x2b", [128, D], F32))
    ln1b = mx.enter_context(nc.sbuf_tensor("ln1b", [128, D], BF16))
    ln1t = mx.enter_context(nc.sbuf_tensor("ln1t", [128, D], BF16))
    ln3b = mx.enter_context(nc.sbuf_tensor("ln3b", [128, D], BF16))
    ln3t = mx.enter_context(nc.sbuf_tensor("ln3t", [128, D], BF16))
    junkS = mx.enter_context(nc.sbuf_tensor("junkS", [128, 5504], BF16))
    junkV = mx.enter_context(nc.sbuf_tensor("junkV", [128, 5504], BF16))
    junkC = mx.enter_context(nc.sbuf_tensor("junkC", [128, 4096], BF16))
    junkL = mx.enter_context(nc.sbuf_tensor("junkL", [128, 4096], BF16))
    p1b = [mx.enter_context(nc.sbuf_tensor(f"p1b{i}", [128, 5504], F32)) for i in range(2)]
    stat = mx.enter_context(nc.sbuf_tensor("stat", [128, 32], F32))

    pmx = contextlib.ExitStack()
    ps_cv = [pmx.enter_context(nc.psum_tensor(f"ps_cv{i}", [128, 512], F32)) for i in range(4)]
    ps_tr = [pmx.enter_context(nc.psum_tensor(f"ps_tr{i}", [128, 128], BF16)) for i in range(2)]

    p1_tiles = []
    slot = 0
    for r in range(32):
        for h2 in range(2):
            p1_tiles.append(("g", wgT[r * 128:(r + 1) * 128, h2 * 5504:(h2 + 1) * 5504], slot)); slot += 1
    for r in range(32):
        for h2 in range(2):
            p1_tiles.append(("u", wuT[r * 128:(r + 1) * 128, h2 * 5504:(h2 + 1) * 5504], slot)); slot += 1
    for r in range(86):
        p1_tiles.append(("d", wdT[r * 128:(r + 1) * 128, :], slot)); slot += 1
    n_p1 = len(p1_tiles)
    SLOT_G = (0, 64)
    SLOT_U = (64, 128)
    SLOT_D = (128, 214)

    p1_marks = {}
    p1_idx = 0

    def emit_p1(n_tiles):
        nonlocal p1_idx
        for _ in range(n_tiles):
            if p1_idx >= n_p1:
                return
            i = p1_idx
            nm, ap_src, sl = p1_tiles[i]
            buf = p1b[i % 2]
            w = ap_src.shape[1]
            if i >= 2:
                wait("sync", *p1_marks[i - 2])
            elif i == 1:
                wait("sync", "d_p1", 16)
            dma("sync", buf[:, 0:w], ap_src, "d_p1")
            eng = "scalar" if i % 2 == 0 else "vector"
            if eng == "scalar":
                wait("scalar", "d_p1", 16 * (i + 1))
                op("scalar", lambda e, buf=buf, w=w, sl=sl: e.activation(
                    junkS[:, 0:w], buf[:, 0:w], AF.Abs,
                    accum_out=p1slot[:, sl:sl + 1]), [("s_mx", 1)])
                p1_marks[i] = ("s_mx", cnt["s_mx"])
            else:
                wait("vector", "d_p1", 16 * (i + 1))
                op("vector", lambda e, buf=buf, w=w, sl=sl: e.reduce_sum(
                    p1slot[:, sl:sl + 1], buf[:, 0:w], AX.X,
                    apply_absolute_value=True), [("v_mx", 1)])
                p1_marks[i] = ("v_mx", cnt["v_mx"])
            p1_idx += 1

    tx_total = cnt["d_tx"]

    def ln_stats(src_sbuf, pref, sq_eng="scalar"):
        op("vector", lambda e: e.reduce_sum(stat[:, 0:1], src_sbuf[:], AX.X), [("v_mx", 1)])
        if sq_eng == "scalar":
            wait("scalar", "s_mx")
        op(sq_eng, lambda e: e.activation(junkL[:, 0:D] if sq_eng == "scalar" else junkV[:, 0:D],
                                          src_sbuf[:], AF.Square,
                                          accum_out=stat[:, 1:2]),
           [("s_mx" if sq_eng == "scalar" else "v_mx", 1)])
        if sq_eng == "scalar":
            wait("vector", "s_mx")
        wait("vector", "v_mx")
        op("vector", lambda e: e.tensor_scalar_mul(stat[:, 2:3], stat[:, 0:1], 1.0 / D), [("v_mx", 1)])
        op("vector", lambda e: e.tensor_scalar_mul(stat[:, 3:4], stat[:, 1:2], 1.0 / D), [("v_mx", 1)])
        wait("vector", "v_mx")
        op("vector", lambda e: e.tensor_scalar(stat[:, 4:5], stat[:, 2:3], stat[:, 2:3], -1.0,
                                               ALU.mult, ALU.mult), [("v_mx", 1)])
        wait("vector", "v_mx")
        op("vector", lambda e: e.tensor_add(stat[:, 4:5], stat[:, 4:5], stat[:, 3:4]), [("v_mx", 1)])
        wait("vector", "v_mx")
        op("vector", lambda e: e.tensor_scalar_add(stat[:, 4:5], stat[:, 4:5], EPS_LN), [("v_mx", 1)])
        wait("scalar", "v_mx")
        op("scalar", lambda e: e.activation(stat[:, 5:6], stat[:, 4:5], AF.Sqrt), [("s_mx", 1)])
        wait("vector", "s_mx")
        op("vector", lambda e: e.reciprocal(stat[:, 6:7], stat[:, 5:6]), [("v_mx", 1)])
        wait("vector", "v_mx")
        op("vector", lambda e: e.tensor_scalar(stat[:, 7:8], stat[:, 2:3], stat[:, 6:7], -1.0,
                                               ALU.mult, ALU.mult), [("v_mx", 1)])
        wait("vector", "v_mx")
        wait("scalar", "v_mx")

    x2_spill_mark = {}
    l3_spill_mark = {}
    for ci in range(NCH):
        u_t = ub[ci % 2]
        if ci >= 2:
            wait("sync", "v_mx")
            wait("sync", "s_mx")
        dma("sync", u_t[:], u_d[ci * 128:(ci + 1) * 128, :], "d_u")
        wait("vector", "d_u", 16 * (ci + 1))
        wait("scalar", "d_u", 16 * (ci + 1))
        ln_stats(u_t, "ln1")
        if ci >= 1:
            wait("scalar", "t_tr")
        op("scalar", lambda e, u_t=u_t: e.activation(ln1b[:], u_t[:], AF.Identity,
                                                     bias=stat[:, 7:8], scale=stat[:, 6:7]),
           [("s_mx", 1)])
        if ci == 0:
            wait("tensor", "d_tx", tx_total)
            wait("tensor", "d_u", 16)
        wait("tensor", "s_mx")
        for jb in range(NJB):
            if not (ci == 0 and jb < 2):
                wait("tensor", "v_mx")
            op("tensor", lambda e, jb=jb: e.transpose(ps_tr[jb % 2][:],
                                                      ln1b[:, jb * 128:(jb + 1) * 128], ident[:]),
               [("t_tr", 1)])
            wait("vector", "t_tr")
            op("vector", lambda e, jb=jb: e.tensor_copy(ln1t[:, jb * 128:(jb + 1) * 128],
                                                        ps_tr[jb % 2][:]), [("v_mx", 1)])
        wait("tensor", "v_mx")
        for s_i in range(NSPAN):
            if s_i >= 4 or ci > 0:
                wait("tensor", "s_ev", cnt["s_ev"] - 3 if cnt["s_ev"] >= 4 else 0)
            jbs = list(range(0, min(NJB, 4 * s_i + 4)))
            for k, jb in enumerate(jbs):
                incs = [("t_cv", 1)] if k == len(jbs) - 1 else []
                op("tensor", lambda e, s_i=s_i, jb=jb, k=k, last=(k == len(jbs) - 1):
                   e.matmul(ps_cv[s_i % 4][:], ln1t[:, jb * 128:(jb + 1) * 128],
                            tallx[:, 512 + 512 * s_i - 128 * jb: 1024 + 512 * s_i - 128 * jb],
                            start=(k == 0), stop=last), incs)
            wait("scalar", "t_cv")
            op("scalar", lambda e, s_i=s_i: e.activation(ysb[:, s_i * 512:(s_i + 1) * 512],
                                                         ps_cv[s_i % 4][:], AF.Copy,
                                                         accum_out=stat[:, 8 + s_i:9 + s_i]),
               [("s_mx", 1)])
            op("scalar", lambda e, s_i=s_i: e.activation(junkC[:, s_i * 512:(s_i + 1) * 512],
                                                         ps_cv[s_i % 4][:], AF.Square,
                                                         accum_out=stat[:, 16 + s_i:17 + s_i]),
               [("s_ev", 1)])
        wait("vector", "s_mx")
        wait("vector", "s_ev")
        op("vector", lambda e: e.reduce_sum(stat[:, 0:1], stat[:, 8:16], AX.X), [("v_mx", 1)])
        op("vector", lambda e: e.reduce_sum(stat[:, 1:2], stat[:, 16:24], AX.X), [("v_mx", 1)])
        wait("vector", "v_mx")
        op("vector", lambda e: e.tensor_scalar_mul(stat[:, 2:3], stat[:, 0:1], 1.0 / D), [("v_mx", 1)])
        op("vector", lambda e: e.tensor_scalar_mul(stat[:, 3:4], stat[:, 1:2], 1.0 / D), [("v_mx", 1)])
        wait("vector", "v_mx")
        op("vector", lambda e: e.tensor_scalar(stat[:, 4:5], stat[:, 2:3], stat[:, 2:3], -1.0,
                                               ALU.mult, ALU.mult), [("v_mx", 1)])
        wait("vector", "v_mx")
        op("vector", lambda e: e.tensor_add(stat[:, 4:5], stat[:, 4:5], stat[:, 3:4]), [("v_mx", 1)])
        wait("vector", "v_mx")
        op("vector", lambda e: e.tensor_scalar_add(stat[:, 4:5], stat[:, 4:5], EPS_LN), [("v_mx", 1)])
        wait("scalar", "v_mx")
        op("scalar", lambda e: e.activation(stat[:, 5:6], stat[:, 4:5], AF.Sqrt), [("s_mx", 1)])
        wait("vector", "s_mx")
        op("vector", lambda e: e.reciprocal(stat[:, 6:7], stat[:, 5:6]), [("v_mx", 1)])
        wait("vector", "v_mx")
        op("vector", lambda e: e.tensor_scalar(stat[:, 7:8], stat[:, 2:3], stat[:, 6:7], -1.0,
                                               ALU.mult, ALU.mult), [("v_mx", 1)])
        wait("vector", "v_mx")
        if ci >= 1:
            wait("vector", "d_sp", x2_spill_mark[ci - 1])
        op("vector", lambda e: e.tensor_scalar(x2b[:], ysb[:], stat[:, 2:3], stat[:, 6:7],
                                               ALU.subtract, ALU.mult), [("v_mx", 1)])
        wait("vector", "v_mx")
        op("vector", lambda e, u_t=u_t: e.tensor_add(x2b[:], x2b[:], u_t[:]), [("v_mx", 1)])
        wait("vector", "v_mx")
        wait("scalar", "v_mx")
        dma("scalar", x2_d[ci], x2b[:], "d_sp")
        x2_spill_mark[ci] = cnt["d_sp"]
        ln_stats(x2b, "ln3")
        wait("scalar", "t_tr")
        op("scalar", lambda e: e.activation(ln3b[:], x2b[:], AF.Identity,
                                            bias=stat[:, 7:8], scale=stat[:, 6:7]), [("s_mx", 1)])
        wait("tensor", "s_mx")
        if ci >= 1:
            wait("vector", "d_sp", l3_spill_mark[ci - 1])
        for jb in range(NJB):
            wait("tensor", "v_mx")
            op("tensor", lambda e, jb=jb: e.transpose(ps_tr[jb % 2][:],
                                                      ln3b[:, jb * 128:(jb + 1) * 128], ident[:]),
               [("t_tr", 1)])
            wait("vector", "t_tr")
            op("vector", lambda e, jb=jb: e.tensor_copy(ln3t[:, jb * 128:(jb + 1) * 128],
                                                        ps_tr[jb % 2][:]), [("v_mx", 1)])
        blk_i, cc = divmod(ci, 4)
        wait("scalar", "v_mx")
        dma("scalar",
            bass.AP(l3t_d.tensor, (blk_i * NJB * 128 * TB) + cc * 128,
                    [[TB, 128], [128 * TB, NJB], [1, 128]]),
            ln3t.rearrange("p (j t) -> p j t", j=NJB),
            "d_sp")
        l3_spill_mark[ci] = cnt["d_sp"]
        emit_p1(14 if ci < 15 else 999)

    pmx2 = contextlib.ExitStack()
    p_sc = pmx2.enter_context(nc.psum_tensor("p_sc", [128, 12], F32))
    p_s1 = pmx2.enter_context(nc.psum_tensor("p_s1", [1, 4], F32))
    wait("vector", "s_mx")
    wait("vector", "v_mx")
    for j, (lo, hi) in enumerate([SLOT_G, SLOT_U, SLOT_D]):
        op("vector", lambda e, lo=lo, hi=hi, j=j: e.reduce_sum(
            stat[:, 24 + j:25 + j], p1slot[:, lo:hi], AX.X), [("v_mx", 1)])
    wait("tensor", "v_mx")
    for j in range(3):
        op("tensor", lambda e, j=j: e.matmul(p_s1[:, j:j + 1], stat[:, 24 + j:25 + j],
                                             ones128c[:], start=True, stop=True), [("t_bc", 1)])
    wait("vector", "t_bc")
    op("vector", lambda e: e.tensor_scalar_mul(svrow[:, 1:4], p_s1[:, 0:3], 1.0 / WG_N), [("v_mx", 1)])
    wait("vector", "v_mx")
    op("vector", lambda e: e.tensor_scalar(svrow[:, 7:10], svrow[:, 1:4], EPS_BL, 0.5,
                                           ALU.add, ALU.mult), [("v_mx", 1)])
    wait("vector", "v_mx")
    op("vector", lambda e: e.tensor_scalar_mul(svrow[:, 4:7], svrow[:, 7:10], -1.0), [("v_mx", 1)])
    op("vector", lambda e: e.tensor_mul(svrow[:, 10:11], svrow[:, 1:2], svrow[:, 2:3]), [("v_mx", 1)])
    wait("vector", "v_mx")
    op("vector", lambda e: e.tensor_scalar_mul(svrow[:, 10:11], svrow[:, 10:11], 0.25), [("v_mx", 1)])
    op("vector", lambda e: e.tensor_scalar_mul(svrow[:, 11:12], svrow[:, 3:4], 0.5), [("v_mx", 1)])
    wait("vector", "v_mx")
    op("vector", lambda e: e.tensor_mul(svrow[:, 12:13], svrow[:, 11:12], svrow[:, 11:12]), [("v_mx", 1)])
    wait("tensor", "v_mx")
    op("tensor", lambda e: e.matmul(p_sc[:], ones128[:], svrow[:, 1:13], start=True, stop=True),
       [("t_bc", 1)])
    wait("vector", "t_bc")
    op("vector", lambda e: e.tensor_copy(cscal[:], p_sc[:]), [("v_mx", 1)])
    NTHR = {"g": cscal[:, 3:4], "u": cscal[:, 4:5], "d": cscal[:, 5:6]}
    THR = {"g": cscal[:, 6:7], "u": cscal[:, 7:8], "d": cscal[:, 8:9]}
    mx_v_done = cnt["v_mx"]
    mx_s_done = cnt["s_mx"]
    mx_t_done = cnt["t_cv"]
    sp_done = cnt["d_sp"]

    pmx2.close()
    pmx.close()
    mx.close()

    ffn = contextlib.ExitStack()
    l3ts = ffn.enter_context(nc.sbuf_tensor("l3ts", [128, NJB * TB], BF16))
    inter = ffn.enter_context(nc.sbuf_tensor("inter", [128, NIB * TB], BF16))
    wgb = [ffn.enter_context(nc.sbuf_tensor(f"wgb{i}", [128, D], BF16)) for i in range(2)]
    wub = [ffn.enter_context(nc.sbuf_tensor(f"wub{i}", [128, D], BF16)) for i in range(2)]
    wdb = [ffn.enter_context(nc.sbuf_tensor(f"wdb{i}", [128, 1024], BF16)) for i in range(2)]
    wf32 = [ffn.enter_context(nc.sbuf_tensor(f"wf32{i}", [128, 1024], F32)) for i in range(2)]
    tqS = [ffn.enter_context(nc.sbuf_tensor(f"tqS{i}", [128, 1024], BF16)) for i in range(2)]
    tqV = [ffn.enter_context(nc.sbuf_tensor(f"tqV{i}", [128, 1024], BF16)) for i in range(2)]
    gsb = [ffn.enter_context(nc.sbuf_tensor(f"gsb{i}", [128, 512], BF16)) for i in range(2)]
    dstg = [ffn.enter_context(nc.sbuf_tensor(f"dstg{i}", [128, 512], F32)) for i in range(2)]
    apy = [ffn.enter_context(nc.sbuf_tensor(f"apy{i}", [128, 512], F32)) for i in range(2)]
    apx = [ffn.enter_context(nc.sbuf_tensor(f"apx{i}", [128, 512], F32)) for i in range(2)]
    fstat = ffn.enter_context(nc.sbuf_tensor("fstat", [128, 192], F32))
    junkF = ffn.enter_context(nc.sbuf_tensor("junkF", [128, 4096], BF16))

    pff = contextlib.ExitStack()
    ps_g = [pff.enter_context(nc.psum_tensor(f"ps_g{i}", [128, 512], F32)) for i in range(2)]
    ps_u = [pff.enter_context(nc.psum_tensor(f"ps_u{i}", [128, 512], F32)) for i in range(2)]
    ps_d = [pff.enter_context(nc.psum_tensor(f"ps_d{i}", [128, 512], F32)) for i in range(4)]
    ps8 = ps_d + ps_g + ps_u

    qs_state = {"last_add": 0}

    def q_s(dst, src, w, mat):
        wait("scalar", "v_q", qs_state["last_add"])
        op("scalar", lambda e: e.activation(tqS[0][:, 0:w], src, AF.Sign, bias=NTHR[mat]),
           [("s_q", 1)])
        op("scalar", lambda e: e.activation(tqS[1][:, 0:w], src, AF.Sign, bias=THR[mat]),
           [("s_q", 1)])
        wait("vector", "s_q")
        op("vector", lambda e: e.tensor_add(dst, tqS[0][:, 0:w], tqS[1][:, 0:w]), [("v_q", 1)])
        qs_state["last_add"] = cnt["v_q"]
        return ("v_q", cnt["v_q"]), ("s_q", cnt["s_q"])

    def q_v(dst, src, w, mat):
        wait("vector", "v_q")
        op("vector", lambda e: e.tensor_scalar(tqV[0][:, 0:w], src, THR[mat], 2.0,
                                               ALU.is_ge, ALU.mult), [("v_q", 1)])
        op("vector", lambda e: e.tensor_scalar(tqV[1][:, 0:w], src, NTHR[mat], 2.0,
                                               ALU.is_le, ALU.mult), [("v_q", 1)])
        wait("vector", "v_q")
        op("vector", lambda e: e.tensor_sub(dst, tqV[0][:, 0:w], tqV[1][:, 0:w]), [("v_q", 1)])
        return ("v_q", cnt["v_q"]), ("v_q", cnt["v_q"])

    wf32_mark = [None, None]

    def quant_1024(dst, src_dram_ap, mat, use_s, dsem="d_wd"):
        bi = quant_1024.tick % 2
        quant_1024.tick += 1
        if wf32_mark[bi] is not None:
            wait("sync", *wf32_mark[bi])
        dma("sync", wf32[bi][:], src_dram_ap, dsem)
        wait("scalar" if use_s else "vector", dsem)
        m = (q_s if use_s else q_v)(dst, wf32[bi][:], 1024, mat)
        wf32_mark[bi] = m[1]
        return m[0]
    quant_1024.tick = 0

    wb_count = {}
    guse = {}
    dstg_mark = {}
    ap_mark = {}

    for blk_i in range(NBLK):
        jit = (blk_i == 0)
        fb = (blk_i % 2) * 96
        if blk_i == 0:
            wait("sync", "d_sp", sp_done)
            wait("sync", "v_mx", mx_v_done)
            wait("sync", "s_mx", mx_s_done)
            wait("sync", "t_cv", mx_t_done)
            wait("vector", "v_mx", mx_v_done)
            wait("scalar", "v_mx", mx_v_done)
            wait("gpsimd", "v_mx", mx_v_done)
        else:
            wait("sync", "t_g", 172 * blk_i)
        dma("sync", l3ts.rearrange("p (j t) -> p j t", j=NJB),
            l3t_d[blk_i].rearrange("j p t -> p j t"), "d_l3")
        l3_target = cnt["d_l3"]

        for i in range(NIB):
            par = i % 2
            if i == 0:
                wait("vector", "t_di", 344 * blk_i)
            qm = {}
            if jit:
                for mat, wl, wq in (("g", wgb, wgq_d), ("u", wub, wuq_d)):
                    if ("wgbuf", mat, par) in guse:
                        wait("vector", "t_g", guse[("wgbuf", mat, par)])
                    if ("wgwb", mat, par) in guse:
                        wait("vector", "d_wb", guse[("wgwb", mat, par)])
                    for qq in range(4):
                        src = bass.AP((wgT if mat == "g" else wuT).tensor,
                                      i * 128 + qq * 8 * 128 * I_DIM,
                                      [[I_DIM, 128], [128 * I_DIM, 8], [1, 128]])
                        m = quant_1024(wl[par][:, qq * 1024:(qq + 1) * 1024],
                                       src, mat, use_s=(qq < 3), dsem="d_w")
                        qm[(mat, qq)] = m
                    wait("gpsimd", *qm[(mat, 3)])
                    wait("gpsimd", *qm[(mat, 2)])
                    dma("gpsimd", wq[i].rearrange("a p f -> p a f"),
                        wl[par][:].rearrange("p (a f) -> p a f", a=32), "d_wb")
                    wb_count[(mat, i)] = cnt["d_wb"]
                    guse[("wgwb", mat, par)] = cnt["d_wb"]
            else:
                for mat, wl, wq in (("g", wgb, wgq_d), ("u", wub, wuq_d)):
                    if ("wgbuf", mat, par) in guse:
                        wait("gpsimd", "t_g", guse[("wgbuf", mat, par)])
                    if ("wgwb", mat, par) in guse:
                        wait("gpsimd", "d_wb", guse[("wgwb", mat, par)])
                        del guse[("wgwb", mat, par)]
                    wait("gpsimd", "d_wb", wb_count[(mat, i)])
                    dma("gpsimd", wl[par][:].rearrange("p (a f) -> p a f", a=32),
                        wq[i].rearrange("a p f -> p a f"), "d_wr")
            if i == 0:
                wait("tensor", "d_l3", l3_target)
                if blk_i == 0:
                    wait("tensor", "v_mx", mx_v_done)
                    wait("tensor", "s_mx", mx_s_done)
                else:
                    wait("tensor", "s_ev")
            if jit:
                wait("tensor", "v_q")
                wait("tensor", "s_q")
            else:
                wait("tensor", "d_wr")
            if ("psg", par) in guse:
                wait("tensor", "s_ev", guse[("psg", par)])
            if ("psu", par) in guse:
                wait("tensor", "v_ff", guse[("psu", par)])
            for psl, wl, mat in ((ps_g, wgb, "g"), (ps_u, wub, "u")):
                for ht in range(32):
                    incs = [("t_g", 1)] if ht == 31 else []
                    op("tensor", lambda e, psl=psl, wl=wl, par=par, ht=ht, last=(ht == 31):
                       e.matmul(psl[par][:], wl[par][:, ht * 128:(ht + 1) * 128],
                                l3ts[:, ht * TB:(ht + 1) * TB],
                                start=(ht == 0), stop=last), incs)
                guse[("wgbuf", mat, par)] = cnt["t_g"]
            wait("scalar", "t_g", cnt["t_g"] - 1)
            if ("gsb", par) in guse:
                wait("scalar", "v_ff", guse[("gsb", par)])
            op("scalar", lambda e, par=par: e.activation(gsb[par][:], ps_g[par][:], AF.Copy,
                                                         scale=cscal[:, 9:10]), [("s_ev", 1)])
            guse[("psg", par)] = cnt["s_ev"]
            wait("vector", "t_g")
            wait("vector", "s_ev")
            op("vector", lambda e, par=par, i=i: e.tensor_mul(
                inter[:, i * TB:(i + 1) * TB], gsb[par][:], ps_u[par][:]), [("v_ff", 1)])
            guse[("psu", par)] = cnt["v_ff"]
            guse[("gsb", par)] = cnt["v_ff"]

        v_ff_all = cnt["v_ff"]
        for spp in range(4):
            for i in range(NIB):
                par = i % 2
                if ("wdb", par) in guse:
                    gk = guse[("wdb", par)]
                    if jit:
                        wait("vector", "t_di", gk)
                        wait("scalar", "t_di", gk)
                        if ("wdwb", par) in guse:
                            wait("vector", "d_wb", guse[("wdwb", par)])
                            wait("scalar", "d_wb", guse[("wdwb", par)])
                    else:
                        wait("gpsimd", "t_di", gk)
                if jit:
                    src = bass.AP(wdT.tensor, i * 128 * D + spp * 1024,
                                  [[D, 128], [1, 1024]])
                    mq = quant_1024(wdb[par][:], src, "d", use_s=(i % 4 < 3), dsem="d_wd")
                    wait("gpsimd", *mq)
                    dma("gpsimd", wdq_d[spp, i], wdb[par][:], "d_wb")
                    wb_count[("d", spp, i)] = cnt["d_wb"]
                    guse[("wdwb", par)] = cnt["d_wb"]
                else:
                    if ("wdwb", par) in guse:
                        wait("gpsimd", "d_wb", guse[("wdwb", par)])
                        del guse[("wdwb", par)]
                    wait("gpsimd", "d_wb", wb_count[("d", spp, i)])
                    dma("gpsimd", wdb[par][:], wdq_d[spp, i], "d_wr")
                if i == 0:
                    if spp == 0:
                        wait("tensor", "v_ff", v_ff_all)
                        wait("tensor", "s_ev")
                    else:
                        wait("tensor", "s_ev")
                if jit:
                    wait("tensor", "v_q")
                    wait("tensor", "s_q")
                else:
                    wait("tensor", "d_wr")
                for ts_i in range(NTSUB):
                    for half in range(2):
                        incs = [("t_di", 1)] if (ts_i == 3 and half == 1) else []
                        op("tensor", lambda e, par=par, i=i, ts_i=ts_i, half=half,
                           last=(i == NIB - 1):
                           e.matmul(ps8[ts_i * 2 + half][:],
                                    inter[:, i * TB + ts_i * 128: i * TB + (ts_i + 1) * 128],
                                    wdb[par][:, half * 512:(half + 1) * 512],
                                    start=(i == 0), stop=last), incs)
                guse[("wdb", par)] = cnt["t_di"]
            wait("scalar", "t_di", cnt["t_di"])
            if blk_i >= 2 and spp == 0:
                wait("scalar", "v_st", guse[("astat", blk_i - 2)])
            for idx in range(8):
                ts_i, half = divmod(idx, 2)
                span = 2 * spp + half
                if ("dstg", idx % 2) in dstg_mark:
                    wait("scalar", "d_sp", dstg_mark[("dstg", idx % 2)])
                op("scalar", lambda e, idx=idx, ts_i=ts_i, span=span, fb=fb: e.activation(
                    dstg[idx % 2][:], ps8[idx][:], AF.Copy, scale=cscal[:, 10:11],
                    accum_out=fstat[:, fb + ts_i * 8 + span: fb + ts_i * 8 + span + 1]),
                   [("s_ev", 1)])
                op("scalar", lambda e, idx=idx, ts_i=ts_i, span=span, fb=fb: e.activation(
                    junkF[:, idx * 512:(idx + 1) * 512], ps8[idx][:], AF.Square,
                    accum_out=fstat[:, fb + 32 + ts_i * 8 + span: fb + 33 + ts_i * 8 + span]),
                   [("s_ev", 1)])
                wait("scalar", "s_ev")
                dma("scalar", dsb_d[blk_i, ts_i, :, span * 512:(span + 1) * 512],
                    dstg[idx % 2][:], "d_sp")
                dstg_mark[("dstg", idx % 2)] = cnt["d_sp"]
        dsb_done = cnt["d_sp"]
        d_ev_done = cnt["s_ev"]

        for ts_i in range(NTSUB):
            ci = blk_i * 4 + ts_i
            wait("vector", "s_ev", d_ev_done)
            op("vector", lambda e, ts_i=ts_i, fb=fb: e.reduce_sum(
                fstat[:, fb + 64 + ts_i:fb + 65 + ts_i],
                fstat[:, fb + ts_i * 8:fb + (ts_i + 1) * 8], AX.X), [("v_st", 1)])
            op("vector", lambda e, ts_i=ts_i, fb=fb: e.reduce_sum(
                fstat[:, fb + 68 + ts_i:fb + 69 + ts_i],
                fstat[:, fb + 32 + ts_i * 8:fb + 32 + (ts_i + 1) * 8], AX.X), [("v_st", 1)])
            wait("vector", "v_st")
            op("vector", lambda e, ts_i=ts_i, fb=fb: e.tensor_scalar_mul(
                fstat[:, fb + 72 + ts_i:fb + 73 + ts_i],
                fstat[:, fb + 64 + ts_i:fb + 65 + ts_i], 1.0 / D), [("v_st", 1)])
            op("vector", lambda e, ts_i=ts_i, fb=fb: e.tensor_scalar(
                fstat[:, fb + 76 + ts_i:fb + 77 + ts_i],
                fstat[:, fb + 68 + ts_i:fb + 69 + ts_i],
                cscal[:, 11:12], 1.0 / D, ALU.mult, ALU.mult), [("v_st", 1)])
            wait("vector", "v_st")
            op("vector", lambda e, ts_i=ts_i, fb=fb: e.tensor_scalar(
                fstat[:, fb + 80 + ts_i:fb + 81 + ts_i],
                fstat[:, fb + 72 + ts_i:fb + 73 + ts_i],
                fstat[:, fb + 72 + ts_i:fb + 73 + ts_i], -1.0, ALU.mult, ALU.mult), [("v_st", 1)])
            wait("vector", "v_st")
            op("vector", lambda e, ts_i=ts_i, fb=fb: e.tensor_add(
                fstat[:, fb + 80 + ts_i:fb + 81 + ts_i],
                fstat[:, fb + 80 + ts_i:fb + 81 + ts_i],
                fstat[:, fb + 76 + ts_i:fb + 77 + ts_i]), [("v_st", 1)])
            wait("vector", "v_st")
            op("vector", lambda e, ts_i=ts_i, fb=fb: e.tensor_scalar_add(
                fstat[:, fb + 80 + ts_i:fb + 81 + ts_i],
                fstat[:, fb + 80 + ts_i:fb + 81 + ts_i], EPS_LN), [("v_st", 1)])
            wait("scalar", "v_st")
            op("scalar", lambda e, ts_i=ts_i, fb=fb: e.activation(
                fstat[:, fb + 84 + ts_i:fb + 85 + ts_i],
                fstat[:, fb + 80 + ts_i:fb + 81 + ts_i], AF.Sqrt), [("s_ff", 1)])
            wait("vector", "s_ff")
            op("vector", lambda e, ts_i=ts_i, fb=fb: e.reciprocal(
                fstat[:, fb + 88 + ts_i:fb + 89 + ts_i],
                fstat[:, fb + 84 + ts_i:fb + 85 + ts_i]), [("v_st", 1)])
            wait("vector", "v_st")
            if ts_i == NTSUB - 1:
                guse[("astat", blk_i)] = cnt["v_st"]
            for hh in range(8):
                bi2 = hh % 2
                if ("ap", bi2) in ap_mark:
                    wait("sync", "v_ap", ap_mark[("ap", bi2)][0])
                    wait("sync", "d_out", ap_mark[("ap", bi2)][1])
                wait("sync", "d_sp", dsb_done)
                dma("sync", apy[bi2][:], dsb_d[blk_i, ts_i, :, hh * 512:(hh + 1) * 512], "d_ap")
                dma("sync", apx[bi2][:], x2_d[ci, :, hh * 512:(hh + 1) * 512], "d_ap")
                wait("vector", "d_ap")
                op("vector", lambda e, bi2=bi2, ts_i=ts_i, fb=fb: e.tensor_scalar(
                    apy[bi2][:], apy[bi2][:], fstat[:, fb + 72 + ts_i:fb + 73 + ts_i],
                    fstat[:, fb + 88 + ts_i:fb + 89 + ts_i], ALU.subtract, ALU.mult), [("v_ap", 1)])
                wait("vector", "v_ap")
                op("vector", lambda e, bi2=bi2: e.tensor_add(apy[bi2][:], apy[bi2][:], apx[bi2][:]),
                   [("v_ap", 1)])
                wait("scalar", "v_ap")
                dma("scalar", out_d[ci * 128:(ci + 1) * 128, hh * 512:(hh + 1) * 512],
                    apy[bi2][:], "d_out")
                ap_mark[("ap", bi2)] = (cnt["v_ap"], cnt["d_out"])

    wait("sync", "d_out", cnt["d_out"])
    wait("gpsimd", "d_wb", cnt["d_wb"])

    with nc.Block() as blk:
        @blk.sync
        def _(e):
            for fn in ops["sync"]:
                fn(e)

        @blk.tensor
        def _(e):
            for fn in ops["tensor"]:
                fn(e)

        @blk.vector
        def _(e):
            for fn in ops["vector"]:
                fn(e)

        @blk.scalar
        def _(e):
            for fn in ops["scalar"]:
                fn(e)

        @blk.gpsimd
        def _(e):
            for fn in ops["gpsimd"]:
                fn(e)

    pff.close()
    ffn.close()
    glob.close()
    st.close()
    return nc


def kernel(L0_attn_u, L0_attn_Lambda_ri, L0_attn_p_ri, L0_attn_q_ri, L0_attn_Vc_ri,
           L0_attn_Ct_ri, L0_attn_Bp, L0_attn_D, L0_attn_log_step,
           L0_ffn_w_gate, L0_ffn_w_up, L0_ffn_w_down):
    u = np.ascontiguousarray(np.asarray(L0_attn_u, dtype=np.float32))
    wgT = np.ascontiguousarray(np.asarray(L0_ffn_w_gate, dtype=np.float32).T)
    wuT = np.ascontiguousarray(np.asarray(L0_ffn_w_up, dtype=np.float32).T)
    wdT = np.ascontiguousarray(np.asarray(L0_ffn_w_down, dtype=np.float32).T)
    lam = np.asarray(L0_attn_Lambda_ri, np.float32)
    p_ri = np.asarray(L0_attn_p_ri, np.float32)
    q_ri = np.asarray(L0_attn_q_ri, np.float32)
    vc = np.asarray(L0_attn_Vc_ri, np.float32)
    ct = np.asarray(L0_attn_Ct_ri, np.float32).reshape(64, 2)
    bp = np.asarray(L0_attn_Bp, np.float32).reshape(64)
    scol = np.stack([lam[:, 0], lam[:, 1], p_ri[:, 0], p_ri[:, 1],
                     q_ri[:, 0], q_ri[:, 1], ct[:, 0], ct[:, 1]], axis=1)
    scol = np.ascontiguousarray(scol, np.float32)
    dls = np.array([[float(np.asarray(L0_attn_D).reshape(-1)[0]),
                     float(np.asarray(L0_attn_log_step).reshape(-1)[0])]], np.float32)
    tabs = _host_tables()

    nc = build_program()
    base = {
        "wgT": wgT, "wuT": wuT, "wdT": wdT,
        "scol": scol, "vcr": np.ascontiguousarray(vc[:, :, 0]),
        "vci": np.ascontiguousarray(vc[:, :, 1]),
        "bprow": bp[None, :], "dls": dls, "ident": tabs["ident"],
        "cvals": np.array([[-1.0, math.log(2.0), 0.0, 0.0]], np.float32),
    }
    for nm in ["t1r", "t1i", "t2r", "t2i", "wa", "wa2", "twr", "twi", "wbc", "wbs",
               "ones1x64", "ones1x128", "ones64x1", "ones128x1"]:
        base[nm] = tabs[nm]
    in_maps = [dict(base, u=np.ascontiguousarray(u[c])) for c in range(8)]
    res = run_bass_kernel_spmd(nc, in_maps, core_ids=list(range(8)))
    out = np.stack([res.results[c]["out"] for c in range(8)], axis=0)
    return out


# version 06 (final)
# speedup vs baseline: 1.0687x; 1.0687x over previous
import math
import contextlib
import numpy as np
import ml_dtypes
import concourse.bass as bass
import concourse.mybir as mybir
from concourse.bass_utils import run_bass_kernel_spmd

F32 = mybir.dt.float32
BF16 = mybir.dt.bfloat16
FP8 = mybir.dt.float8e4
AF = mybir.ActivationFunctionType
ALU = mybir.AluOpType
AX = mybir.AxisListType

D = 4096
TOK = 2048
NCH = 16
NJB = D // 128
NSPAN = D // 512
I_DIM = 11008
NIB = I_DIM // 128
TB = 512
NBLK = TOK // TB
NTSUB = TB // 128
KPLEN = 128 + 4608
TXW = 4608
N_STATE = 64
EPS_BL = 1e-5
EPS_LN = 1e-5
WG_N = I_DIM * D


def _host_tables():
    L = D
    l = np.arange(L)
    Om = np.exp(-2j * np.pi * l / L)
    T1 = (1.0 - Om) / (1.0 + Om)
    T2 = 2.0 / (1.0 + Om)
    t = {}
    t["t1r"] = T1.real.astype(np.float32)[None, :]
    t["t1i"] = T1.imag.astype(np.float32)[None, :]
    t["t2r"] = T2.real.astype(np.float32)[None, :]
    t["t2i"] = T2.imag.astype(np.float32)[None, :]
    l1 = np.arange(64)[:, None]
    n0 = np.arange(64)[None, :]
    cA = np.cos(2 * np.pi * n0 * l1 / 64).astype(np.float32)
    sA = np.sin(2 * np.pi * n0 * l1 / 64).astype(np.float32)
    t["wa"] = np.concatenate([cA, sA], axis=1)
    t["wa2"] = np.concatenate([-sA, cA], axis=1)
    l0 = np.arange(64)[:, None]
    t["twr"] = (np.cos(2 * np.pi * n0 * l0 / 4096) / 4096.0).astype(np.float32)
    t["twi"] = (np.sin(2 * np.pi * n0 * l0 / 4096) / 4096.0).astype(np.float32)
    n1 = np.arange(64)[None, :]
    t["wbc"] = np.cos(2 * np.pi * n1 * l0 / 64).astype(np.float32)
    t["wbs"] = (-np.sin(2 * np.pi * n1 * l0 / 64)).astype(np.float32)
    t["ident"] = np.eye(128, dtype=np.float32).astype(ml_dtypes.bfloat16)
    t["ones1x64"] = np.ones((1, 64), np.float32)
    t["ones1x128"] = np.ones((1, 128), np.float32)
    t["ones64x1"] = np.ones((64, 1), np.float32)
    t["ones128x1"] = np.ones((128, 1), np.float32)
    return t


def build_program():
    nc = bass.Bass("TRN2", target_bir_lowering=False)

    u_d = nc.dram_tensor("u", [TOK, D], F32, kind="ExternalInput").ap()
    wgT = nc.dram_tensor("wgT", [D, I_DIM], F32, kind="ExternalInput").ap()
    wuT = nc.dram_tensor("wuT", [D, I_DIM], F32, kind="ExternalInput").ap()
    wdT = nc.dram_tensor("wdT", [I_DIM, D], F32, kind="ExternalInput").ap()
    scol = nc.dram_tensor("scol", [64, 8], F32, kind="ExternalInput").ap()
    vcr_d = nc.dram_tensor("vcr", [64, 64], F32, kind="ExternalInput").ap()
    vci_d = nc.dram_tensor("vci", [64, 64], F32, kind="ExternalInput").ap()
    bprow_d = nc.dram_tensor("bprow", [1, 64], F32, kind="ExternalInput").ap()
    dls_d = nc.dram_tensor("dls", [1, 2], F32, kind="ExternalInput").ap()
    cvals_d = nc.dram_tensor("cvals", [1, 4], F32, kind="ExternalInput").ap()
    tabs = {}
    for nm, shp in [("t1r", [1, D]), ("t1i", [1, D]), ("t2r", [1, D]), ("t2i", [1, D]),
                    ("wa", [64, 128]), ("wa2", [64, 128]), ("twr", [64, 64]),
                    ("twi", [64, 64]), ("wbc", [64, 64]), ("wbs", [64, 64]),
                    ("ones1x64", [1, 64]), ("ones1x128", [1, 128]),
                    ("ones64x1", [64, 1]), ("ones128x1", [128, 1])]:
        tabs[nm] = nc.dram_tensor(nm, shp, F32, kind="ExternalInput").ap()
    ident_d = nc.dram_tensor("ident", [128, 128], BF16, kind="ExternalInput").ap()
    out_d = nc.dram_tensor("out", [TOK, D], F32, kind="ExternalOutput").ap()

    kp_d = nc.dram_tensor("kp", [KPLEN], BF16, kind="Internal")
    bounce_d = nc.dram_tensor("bounce", [16, 4096], F32, kind="Internal").ap()
    wgq_d = nc.dram_tensor("wgq", [NIB, 32, 128, 128], FP8, kind="Internal").ap()
    wuq_d = nc.dram_tensor("wuq", [NIB, 32, 128, 128], FP8, kind="Internal").ap()
    wdq_d = nc.dram_tensor("wdq", [2, NIB, 128, 2048], FP8, kind="Internal").ap()
    l3t_d = nc.dram_tensor("l3t", [NBLK, NJB, 128, TB], BF16, kind="Internal").ap()
    x2_d = nc.dram_tensor("x2sp", [NCH, 128, D], F32, kind="Internal").ap()
    dsb_d = nc.dram_tensor("dsb", [NBLK, NTSUB, 128, D], F32, kind="Internal").ap()

    st = contextlib.ExitStack()
    sems = {}
    for nm in ["d_small", "d_u", "d_p1", "d_tx", "d_sp", "d_l3", "d_w", "d_wd",
               "d_ap", "d_out", "d_wb", "d_bn", "d_wr", "d_p1v", "p_mx", "d_wrv", "d_wds", "d_w2", "p_q", "t_s4", "t_tr", "t_cv", "t_g",
               "t_d", "t_di", "t_bc", "v_s4", "v_mx", "s_mx", "v_ff", "s_ff", "v_q",
               "s_q", "v_st", "s_ev", "v_ap"]:
        sems[nm] = st.enter_context(nc.semaphore(nm))
    cnt = {nm: 0 for nm in sems}

    ops = {"sync": [], "tensor": [], "vector": [], "scalar": [], "gpsimd": []}

    def em(e, fn):
        ops[e].append(fn)

    def inc_of(instr, pairs):
        for s, n in pairs:
            instr = instr.then_inc(sems[s], n)
        return instr

    def dma(e, dst, src, s, **kw):
        em(e, lambda eng, s=s, v=cnt[s]: eng.wait_ge(sems[s], v))
        cnt[s] += 16
        em(e, lambda eng, dst=dst, src=src, kw=kw, s=s:
           eng.dma_start(out=dst, in_=src, **kw).then_inc(sems[s], 16))

    def wait(e, s, val=None):
        v = cnt[s] if val is None else val
        em(e, lambda eng, s=s, v=v: eng.wait_ge(sems[s], v))

    def op(e, f, incs=()):
        for s, n in incs:
            cnt[s] += n
        em(e, lambda eng, f=f, incs=tuple(incs): inc_of(f(eng), incs))

    glob = contextlib.ExitStack()
    tallx = glob.enter_context(nc.sbuf_tensor("tallx", [128, TXW], BF16))
    ident = glob.enter_context(nc.sbuf_tensor("ident_s", [128, 128], BF16))
    ones128 = glob.enter_context(nc.sbuf_tensor("ones128", [1, 128], F32))
    ones64c = glob.enter_context(nc.sbuf_tensor("ones64c", [64, 1], F32))
    ones128c = glob.enter_context(nc.sbuf_tensor("ones128c", [128, 1], F32))
    cscal = glob.enter_context(nc.sbuf_tensor("cscal", [128, 12], F32))
    p1slot = glob.enter_context(nc.sbuf_tensor("p1slot", [128, 288], F32))
    svrow = glob.enter_context(nc.sbuf_tensor("svrow", [1, 16], F32))
    cvals = glob.enter_context(nc.sbuf_tensor("cvals_s", [1, 4], F32))

    dma("sync", ident[:], ident_d, "d_small")
    dma("sync", ones128[:], tabs["ones1x128"], "d_small")
    dma("sync", ones64c[:], tabs["ones64x1"], "d_small")
    dma("sync", ones128c[:], tabs["ones128x1"], "d_small")
    dma("sync", cvals[:], cvals_d, "d_small")

    s4 = contextlib.ExitStack()
    b = [s4.enter_context(nc.sbuf_tensor(f"s4b{i}", [64, D], F32)) for i in range(6)]
    vcr = s4.enter_context(nc.sbuf_tensor("vcr_s", [64, 64], F32))
    vci = s4.enter_context(nc.sbuf_tensor("vci_s", [64, 64], F32))
    bpr = s4.enter_context(nc.sbuf_tensor("bpr_s", [1, 64], F32))
    scs = s4.enter_context(nc.sbuf_tensor("scs", [64, 8], F32))
    dls = s4.enter_context(nc.sbuf_tensor("dls_s", [1, 2], F32))
    numc = s4.enter_context(nc.sbuf_tensor("numc", [64, 10], F32))
    t2rows = s4.enter_context(nc.sbuf_tensor("t2rows", [1, 2 * D], F32))
    krow_st = s4.enter_context(nc.sbuf_tensor("krow_st", [1, 512], F32))
    kc = s4.enter_context(nc.sbuf_tensor("kc", [64, 2 * 64], F32))
    karg = s4.enter_context(nc.sbuf_tensor("karg", [64, 4 * 64], F32))
    kmod = s4.enter_context(nc.sbuf_tensor("kmod", [64, 64], F32))
    kmodb = s4.enter_context(nc.sbuf_tensor("kmodb", [64, 64], BF16))
    kmodbT = s4.enter_context(nc.sbuf_tensor("kmodbT", [64, 64], BF16))
    zrow = s4.enter_context(nc.sbuf_tensor("zrow", [1, 640], BF16))
    was = s4.enter_context(nc.sbuf_tensor("was", [64, 128], F32))
    wa2s = s4.enter_context(nc.sbuf_tensor("wa2s", [64, 128], F32))
    twrs = s4.enter_context(nc.sbuf_tensor("twrs", [64, 64], F32))
    twis = s4.enter_context(nc.sbuf_tensor("twis", [64, 64], F32))
    wbcs = s4.enter_context(nc.sbuf_tensor("wbcs", [64, 64], F32))
    wbss = s4.enter_context(nc.sbuf_tensor("wbss", [64, 64], F32))
    ones64r = s4.enter_context(nc.sbuf_tensor("ones64r", [1, 64], F32))

    ps4 = contextlib.ExitStack()
    p_rep = ps4.enter_context(nc.psum_tensor("p_rep", [64, 512], F32))
    p_row = ps4.enter_context(nc.psum_tensor("p_row", [1, 512], F32))
    p_A = ps4.enter_context(nc.psum_tensor("p_A", [64, 128], F32))
    p_K = ps4.enter_context(nc.psum_tensor("p_K", [64, 64], F32))
    p_Kt = ps4.enter_context(nc.psum_tensor("p_Kt", [64, 64], BF16))
    p_bc = ps4.enter_context(nc.psum_tensor("p_bc", [128, 12], F32))

    for nm, dst in [("wa", was), ("wa2", wa2s), ("twr", twrs), ("twi", twis),
                    ("wbc", wbcs), ("wbs", wbss), ("ones1x64", ones64r)]:
        dma("sync", dst[:], tabs[nm], "d_small")
    dma("sync", vcr[:], vcr_d, "d_small")
    dma("sync", vci[:], vci_d, "d_small")
    dma("sync", bpr[:], bprow_d, "d_small")
    dma("sync", scs[:], scol, "d_small")
    dma("sync", dls[:], dls_d, "d_small")
    dma("sync", t2rows[:, 0:D], tabs["t2r"], "d_small")
    dma("sync", t2rows[:, D:2 * D], tabs["t2i"], "d_small")
    t1rows = s4.enter_context(nc.sbuf_tensor("t1rows", [1, 2 * D], F32))
    dma("sync", t1rows[:, 0:D], tabs["t1r"], "d_small")
    dma("sync", t1rows[:, D:2 * D], tabs["t1i"], "d_small")

    wait("tensor", "d_small")
    wait("vector", "d_small")
    wait("scalar", "d_small")

    op("scalar", lambda e: e.activation(svrow[:, 0:1], dls[:, 1:2], AF.Exp,
                                        scale=cvals[:, 0:1], bias=cvals[:, 1:2]), [("s_q", 1)])
    wait("tensor", "s_q")
    op("tensor", lambda e: e.matmul(p_bc[0:64, 0:1], ones64r[:], svrow[:, 0:1],
                                    start=True, stop=True), [("t_bc", 1)])
    cgcol = numc[:, 0:1]
    cgc = s4.enter_context(nc.sbuf_tensor("cgc", [64, 1], F32))
    wait("vector", "t_bc")
    op("vector", lambda e: e.tensor_copy(cgc[:], p_bc[0:64, 0:1]), [("v_s4", 1)])

    op("tensor", lambda e: e.matmul(p_rep[:, 0:64], ones64r[:], bpr[:],
                                    start=True, stop=True), [("t_bc", 1)])
    wait("vector", "t_bc")
    op("vector", lambda e: e.tensor_mul(kc[:, 0:64], vcr[:], p_rep[:, 0:64]), [("v_s4", 1)])
    op("vector", lambda e: e.tensor_mul(kc[:, 64:128], vci[:], p_rep[:, 0:64]), [("v_s4", 1)])
    wait("vector", "v_s4")
    op("vector", lambda e: e.reduce_sum(numc[:, 0:1], kc[:, 0:64], AX.X), [("v_s4", 1)])
    op("vector", lambda e: e.reduce_sum(numc[:, 1:2], kc[:, 64:128], AX.X), [("v_s4", 1)])
    wait("vector", "v_s4")
    def cplx_conj_mul(dst_r, dst_i, xr, xi, yr, yi):
        op("vector", lambda e: e.tensor_mul(kc[:, 0:1], xr, yr), [("v_s4", 1)])
        op("vector", lambda e: e.tensor_mul(kc[:, 1:2], xi, yi), [("v_s4", 1)])
        op("vector", lambda e: e.tensor_mul(kc[:, 2:3], xr, yi), [("v_s4", 1)])
        op("vector", lambda e: e.tensor_mul(kc[:, 3:4], xi, yr), [("v_s4", 1)])
        wait("vector", "v_s4")
        op("vector", lambda e: e.tensor_add(dst_r, kc[:, 0:1], kc[:, 1:2]), [("v_s4", 1)])
        op("vector", lambda e: e.tensor_sub(dst_i, kc[:, 2:3], kc[:, 3:4]), [("v_s4", 1)])
        wait("vector", "v_s4")

    cplx_conj_mul(numc[:, 2:3], numc[:, 3:4], scs[:, 6:7], scs[:, 7:8], numc[:, 0:1], numc[:, 1:2])
    cplx_conj_mul(numc[:, 4:5], numc[:, 5:6], scs[:, 6:7], scs[:, 7:8], scs[:, 2:3], scs[:, 3:4])
    cplx_conj_mul(numc[:, 6:7], numc[:, 7:8], scs[:, 4:5], scs[:, 5:6], numc[:, 0:1], numc[:, 1:2])
    cplx_conj_mul(numc[:, 8:9], numc[:, 9:10], scs[:, 4:5], scs[:, 5:6], scs[:, 2:3], scs[:, 3:4])
    nnum = s4.enter_context(nc.sbuf_tensor("nnum", [64, 4], F32))
    for j in range(4):
        op("vector", lambda e, j=j: e.tensor_scalar_mul(nnum[:, j:j + 1],
                                                        numc[:, 2 + 2 * j:3 + 2 * j], -1.0), [("v_s4", 1)])
    wait("vector", "v_s4")

    wait("tensor", "v_s4")
    nbc = 0
    for comp, buf in ((0, b[0]), (1, b[1])):
        for ch in range(8):
            op("tensor", lambda e, comp=comp, ch=ch: e.matmul(
                p_rep[:], ones64r[:], t1rows[:, comp * D + ch * 512: comp * D + (ch + 1) * 512],
                start=True, stop=True), [("t_bc", 1)])
            nbc += 1
            wait("vector", "t_bc")
            op("vector", lambda e, buf=buf, ch=ch: e.tensor_copy(
                buf[:, ch * 512:(ch + 1) * 512], p_rep[:]), [("v_s4", 1)])
            wait("tensor", "v_s4")

    wait("vector", "v_s4")
    op("vector", lambda e: e.tensor_scalar(b[2][:], b[0][:], cgc[:], scs[:, 0:1],
                                           ALU.mult, ALU.subtract), [("v_s4", 1)])
    op("vector", lambda e: e.tensor_scalar(b[3][:], b[1][:], cgc[:], scs[:, 1:2],
                                           ALU.mult, ALU.subtract), [("v_s4", 1)])
    wait("vector", "v_s4")
    op("vector", lambda e: e.tensor_mul(b[4][:], b[2][:], b[2][:]), [("v_s4", 1)])
    op("vector", lambda e: e.tensor_mul(b[5][:], b[3][:], b[3][:]), [("v_s4", 1)])
    wait("vector", "v_s4")
    op("vector", lambda e: e.tensor_add(b[4][:], b[4][:], b[5][:]), [("v_s4", 1)])
    wait("vector", "v_s4")
    op("vector", lambda e: e.reciprocal(b[5][:], b[4][:]), [("v_s4", 1)])
    wait("vector", "v_s4")
    op("vector", lambda e: e.tensor_mul(b[0][:], b[2][:], b[5][:]), [("v_s4", 1)])
    op("vector", lambda e: e.tensor_mul(b[1][:], b[3][:], b[5][:]), [("v_s4", 1)])
    wait("vector", "v_s4")

    for j in range(4):
        nr = numc[:, 2 + 2 * j:3 + 2 * j]
        ni = numc[:, 3 + 2 * j:4 + 2 * j]
        nnr = nnum[:, j:j + 1]
        op("vector", lambda e, nr=nr: e.tensor_scalar_mul(b[2][:], b[0][:], nr), [("v_s4", 1)])
        op("vector", lambda e, ni=ni: e.tensor_scalar_mul(b[3][:], b[0][:], ni), [("v_s4", 1)])
        wait("vector", "v_s4")
        op("vector", lambda e, ni=ni: e.scalar_tensor_tensor(b[2][:], b[1][:], ni, b[2][:],
                                                             ALU.mult, ALU.add), [("v_s4", 1)])
        op("vector", lambda e, nnr=nnr: e.scalar_tensor_tensor(b[3][:], b[1][:], nnr, b[3][:],
                                                               ALU.mult, ALU.add), [("v_s4", 1)])
        wait("vector", "v_s4")
        for comp, buf in ((2 * j, b[2]), (2 * j + 1, b[3])):
            for ch in range(8):
                wait("tensor", "v_s4")
                op("tensor", lambda e, buf=buf, ch=ch: e.matmul(
                    p_row[:], ones64c[:], buf[:, ch * 512:(ch + 1) * 512],
                    start=True, stop=True), [("t_bc", 1)])
                wait("vector", "t_bc")
                op("vector", lambda e, ch=ch: e.tensor_copy(krow_st[:], p_row[:]), [("v_s4", 1)])
                wait("sync", "v_s4")
                dma("sync", bounce_d[comp, ch * 512:(ch + 1) * 512], krow_st[:], "d_bn")
                wait("vector", "d_bn")
                wait("tensor", "d_bn")

    k8 = s4.enter_context(nc.sbuf_tensor("k8", [64, 8 * 64], F32))
    wait("sync", "d_bn")
    for comp in range(8):
        dma("sync", k8[:, comp * 64:(comp + 1) * 64],
            bounce_d[comp, :].rearrange("(a b) -> a b", a=64), "d_bn")
    wait("sync", "d_bn")
    wait("vector", "d_bn")
    KR = lambda c: k8[:, c * 64:(c + 1) * 64]
    tA = s4.enter_context(nc.sbuf_tensor("tA", [64, 6 * 64], F32))
    TT = lambda c: tA[:, c * 64:(c + 1) * 64]
    op("vector", lambda e: e.tensor_mul(TT(0), KR(2), KR(4)), [("v_s4", 1)])
    op("vector", lambda e: e.tensor_mul(TT(1), KR(3), KR(5)), [("v_s4", 1)])
    op("vector", lambda e: e.tensor_mul(TT(2), KR(2), KR(5)), [("v_s4", 1)])
    op("vector", lambda e: e.tensor_mul(TT(3), KR(3), KR(4)), [("v_s4", 1)])
    wait("vector", "v_s4")
    op("vector", lambda e: e.tensor_sub(TT(0), TT(0), TT(1)), [("v_s4", 1)])
    wait("vector", "v_s4")
    op("vector", lambda e: e.tensor_add(TT(1), TT(2), TT(3)), [("v_s4", 1)])
    wait("vector", "v_s4")
    op("vector", lambda e: e.tensor_scalar_add(TT(2), KR(6), 1.0), [("v_s4", 1)])
    wait("vector", "v_s4")
    op("vector", lambda e: e.tensor_mul(TT(4), TT(2), TT(2)), [("v_s4", 1)])
    op("vector", lambda e: e.tensor_mul(TT(5), KR(7), KR(7)), [("v_s4", 1)])
    wait("vector", "v_s4")
    op("vector", lambda e: e.tensor_add(TT(4), TT(4), TT(5)), [("v_s4", 1)])
    wait("vector", "v_s4")
    op("vector", lambda e: e.reciprocal(TT(5), TT(4)), [("v_s4", 1)])
    wait("vector", "v_s4")
    op("vector", lambda e: e.tensor_mul(TT(4), TT(0), TT(2)), [("v_s4", 1)])
    wait("vector", "v_s4")
    op("vector", lambda e: e.tensor_mul(kc[:, 0:64], TT(1), KR(7)), [("v_s4", 1)])
    wait("vector", "v_s4")
    op("vector", lambda e: e.tensor_add(TT(4), TT(4), kc[:, 0:64]), [("v_s4", 1)])
    wait("vector", "v_s4")
    op("vector", lambda e: e.tensor_mul(TT(4), TT(4), TT(5)), [("v_s4", 1)])
    op("vector", lambda e: e.tensor_mul(kc[:, 0:64], TT(1), TT(2)), [("v_s4", 1)])
    op("vector", lambda e: e.tensor_mul(kc[:, 64:128], TT(0), KR(7)), [("v_s4", 1)])
    wait("vector", "v_s4")
    op("vector", lambda e: e.tensor_sub(kc[:, 0:64], kc[:, 0:64], kc[:, 64:128]), [("v_s4", 1)])
    wait("vector", "v_s4")
    op("vector", lambda e: e.tensor_mul(TT(5), kc[:, 0:64], TT(5)), [("v_s4", 1)])
    wait("vector", "v_s4")
    op("vector", lambda e: e.tensor_sub(TT(0), KR(0), TT(4)), [("v_s4", 1)])
    op("vector", lambda e: e.tensor_sub(TT(1), KR(1), TT(5)), [("v_s4", 1)])
    wait("vector", "v_s4")
    t2t = s4.enter_context(nc.sbuf_tensor("t2t", [64, 128], F32))
    dma("sync", bounce_d[8, :], t2rows[:, 0:D], "d_bn")
    dma("sync", bounce_d[9, :], t2rows[:, D:2 * D], "d_bn")
    wait("sync", "d_bn")
    dma("sync", t2t[:, 0:64], bounce_d[8, :].rearrange("(a b) -> a b", a=64), "d_bn")
    dma("sync", t2t[:, 64:128], bounce_d[9, :].rearrange("(a b) -> a b", a=64), "d_bn")
    wait("vector", "d_bn")
    op("vector", lambda e: e.tensor_mul(TT(2), t2t[:, 0:64], TT(0)), [("v_s4", 1)])
    op("vector", lambda e: e.tensor_mul(TT(3), t2t[:, 64:128], TT(1)), [("v_s4", 1)])
    op("vector", lambda e: e.tensor_mul(TT(4), t2t[:, 0:64], TT(1)), [("v_s4", 1)])
    op("vector", lambda e: e.tensor_mul(TT(5), t2t[:, 64:128], TT(0)), [("v_s4", 1)])
    wait("vector", "v_s4")
    op("vector", lambda e: e.tensor_sub(karg[:, 0:64], TT(2), TT(3)), [("v_s4", 1)])
    op("vector", lambda e: e.tensor_add(karg[:, 64:128], TT(4), TT(5)), [("v_s4", 1)])
    wait("vector", "v_s4")

    wait("tensor", "v_s4")
    op("tensor", lambda e: e.matmul(p_A[:], karg[:, 0:64], was[:], start=True, stop=False))
    op("tensor", lambda e: e.matmul(p_A[:], karg[:, 64:128], wa2s[:], start=False, stop=True),
       [("t_s4", 1)])
    wait("vector", "t_s4")
    op("vector", lambda e: e.tensor_copy(karg[:, 128:256], p_A[:]), [("v_s4", 1)])
    wait("vector", "v_s4")
    Ar = karg[:, 128:192]
    Ai = karg[:, 192:256]
    op("vector", lambda e: e.tensor_mul(TT(0), Ar, twrs[:]), [("v_s4", 1)])
    op("vector", lambda e: e.tensor_mul(TT(1), Ai, twis[:]), [("v_s4", 1)])
    op("vector", lambda e: e.tensor_mul(TT(2), Ar, twis[:]), [("v_s4", 1)])
    op("vector", lambda e: e.tensor_mul(TT(3), Ai, twrs[:]), [("v_s4", 1)])
    wait("vector", "v_s4")
    op("vector", lambda e: e.tensor_sub(TT(0), TT(0), TT(1)), [("v_s4", 1)])
    wait("vector", "v_s4")
    op("vector", lambda e: e.tensor_add(TT(1), TT(2), TT(3)), [("v_s4", 1)])
    wait("vector", "v_s4")
    wait("tensor", "v_s4")
    op("tensor", lambda e: e.matmul(p_K[:], TT(0), wbcs[:], start=True, stop=False))
    op("tensor", lambda e: e.matmul(p_K[:], TT(1), wbss[:], start=False, stop=True),
       [("t_s4", 1)])
    wait("vector", "t_s4")
    op("vector", lambda e: e.tensor_copy(kmod[:], p_K[:]), [("v_s4", 1)])
    wait("vector", "v_s4")
    op("vector", lambda e: e.tensor_add(kmod[0:1, 0:1], kmod[0:1, 0:1], dls[:, 0:1]), [("v_s4", 1)])
    wait("vector", "v_s4")
    op("vector", lambda e: e.tensor_copy(kmodb[:], kmod[:]), [("v_s4", 1)])
    wait("tensor", "v_s4")
    op("tensor", lambda e: e.transpose(p_Kt[:], kmodb[:], ident[0:64, 0:64]), [("t_s4", 1)])
    wait("vector", "t_s4")
    op("vector", lambda e: e.tensor_copy(kmodbT[:], p_Kt[:]), [("v_s4", 1)])
    op("vector", lambda e: e.memset(zrow[:], 0.0), [("v_s4", 1)])
    wait("sync", "v_s4")
    dma("sync", bass.AP(kp_d, 0, [[1, 1], [1, 640]]), zrow[:], "d_tx")
    dma("sync", bass.AP(kp_d, 640, [[64, 64], [1, 64]]), kmodbT[:], "d_tx")
    wait("sync", "d_tx")
    wait("scalar", "d_tx")
    for p in range(128):
        q = "sync" if p % 2 == 0 else "scalar"
        dma(q, tallx[p:p + 1, :], bass.AP(kp_d, 128 - p, [[1, 1], [1, TXW]]), "d_tx")

    ps4.close()
    s4.close()

    for _eng in ("gpsimd", "sync", "scalar"):
        for _s in ("v_s4", "t_s4", "t_bc", "s_q", "d_bn", "d_tx", "d_small"):
            wait(_eng, _s)

    mx = contextlib.ExitStack()
    ub = [mx.enter_context(nc.sbuf_tensor(f"ub{i}", [128, D], F32)) for i in range(2)]
    ysb = mx.enter_context(nc.sbuf_tensor("ysb", [128, D], F32))
    x2b = mx.enter_context(nc.sbuf_tensor("x2b", [128, D], F32))
    ln1b = mx.enter_context(nc.sbuf_tensor("ln1b", [128, D], BF16))
    ln1t = mx.enter_context(nc.sbuf_tensor("ln1t", [128, D], BF16))
    ln3b = mx.enter_context(nc.sbuf_tensor("ln3b", [128, D], BF16))
    ln3t = mx.enter_context(nc.sbuf_tensor("ln3t", [128, D], BF16))
    junkS = mx.enter_context(nc.sbuf_tensor("junkS", [128, 4096], BF16))
    junkC = mx.enter_context(nc.sbuf_tensor("junkC", [128, 4096], BF16))
    junkL = mx.enter_context(nc.sbuf_tensor("junkL", [128, 4096], BF16))
    p1b = [mx.enter_context(nc.sbuf_tensor(f"p1b{i}", [128, 4096], F32)) for i in range(3)]
    stat = mx.enter_context(nc.sbuf_tensor("stat", [128, 32], F32))

    pmx = contextlib.ExitStack()
    ps_cv = [pmx.enter_context(nc.psum_tensor(f"ps_cv{i}", [128, 512], F32)) for i in range(4)]
    ps_tr = [pmx.enter_context(nc.psum_tensor(f"ps_tr{i}", [128, 128], BF16)) for i in range(2)]

    p1_tiles = []
    slot = 0
    for nm, t in (("g", wgT), ("u", wuT)):
        for r in range(32):
            for c0, w in ((0, 4096), (4096, 4096), (8192, 2816)):
                p1_tiles.append((nm, t[r * 128:(r + 1) * 128, c0:c0 + w], slot)); slot += 1
    SLOT_G = (0, 96)
    SLOT_U = (96, 192)
    for r in range(86):
        p1_tiles.append(("d" + ("s" if r % 2 == 0 else "v"),
                         wdT[r * 128:(r + 1) * 128, :], slot)); slot += 1
    SLOT_D = (192, 278)
    n_p1 = len(p1_tiles)

    p1_marks = {}
    p1_idx = 0

    def emit_p1(n_tiles):
        nonlocal p1_idx
        for _ in range(n_tiles):
            if p1_idx >= n_p1:
                return
            i = p1_idx
            nm, ap_src, sl = p1_tiles[i]
            buf = p1b[i % 3]
            w = ap_src.shape[1]
            q, dsem = (("sync", "d_p1") if i % 2 == 0 else ("gpsimd", "d_p1v"))
            if i >= 3:
                wait(q, *p1_marks[i - 3])
            dma(q, buf[:, 0:w], ap_src, dsem)
            ldv = cnt[dsem]
            eng = {"g": "scalar", "u": "vector", "ds": "scalar", "dv": "vector"}[nm]
            if eng == "scalar":
                wait("scalar", dsem, ldv)
                wait("scalar", "s_mx")
                op("scalar", lambda e, buf=buf, w=w, sl=sl: e.activation(
                    junkS[:, 0:w], buf[:, 0:w], AF.Abs,
                    accum_out=p1slot[:, sl:sl + 1]), [("s_mx", 1)])
                p1_marks[i] = ("s_mx", cnt["s_mx"])
            elif eng == "vector":
                wait("vector", dsem, ldv)
                op("vector", lambda e, buf=buf, w=w, sl=sl: e.reduce_sum(
                    p1slot[:, sl:sl + 1], buf[:, 0:w], AX.X,
                    apply_absolute_value=True), [("v_mx", 1)])
                p1_marks[i] = ("v_mx", cnt["v_mx"])
            p1_idx += 1

    tx_total = cnt["d_tx"]

    def ln_stats(src_sbuf, pref, sq_eng="scalar"):
        op("vector", lambda e: e.reduce_sum(stat[:, 0:1], src_sbuf[:], AX.X), [("v_mx", 1)])
        if sq_eng == "scalar":
            wait("scalar", "s_mx")
        op(sq_eng, lambda e: e.activation(junkL[:, 0:D],
                                          src_sbuf[:], AF.Square,
                                          accum_out=stat[:, 1:2]),
           [("s_mx", 1)])
        if sq_eng == "scalar":
            wait("vector", "s_mx")
        wait("vector", "v_mx")
        op("vector", lambda e: e.tensor_scalar_mul(stat[:, 2:3], stat[:, 0:1], 1.0 / D), [("v_mx", 1)])
        op("vector", lambda e: e.tensor_scalar_mul(stat[:, 3:4], stat[:, 1:2], 1.0 / D), [("v_mx", 1)])
        wait("vector", "v_mx")
        op("vector", lambda e: e.tensor_scalar(stat[:, 4:5], stat[:, 2:3], stat[:, 2:3], -1.0,
                                               ALU.mult, ALU.mult), [("v_mx", 1)])
        wait("vector", "v_mx")
        op("vector", lambda e: e.tensor_add(stat[:, 4:5], stat[:, 4:5], stat[:, 3:4]), [("v_mx", 1)])
        wait("vector", "v_mx")
        op("vector", lambda e: e.tensor_scalar_add(stat[:, 4:5], stat[:, 4:5], EPS_LN), [("v_mx", 1)])
        wait("scalar", "v_mx")
        op("scalar", lambda e: e.activation(stat[:, 5:6], stat[:, 4:5], AF.Sqrt), [("s_mx", 1)])
        wait("vector", "s_mx")
        op("vector", lambda e: e.reciprocal(stat[:, 6:7], stat[:, 5:6]), [("v_mx", 1)])
        wait("vector", "v_mx")
        op("vector", lambda e: e.tensor_scalar(stat[:, 7:8], stat[:, 2:3], stat[:, 6:7], -1.0,
                                               ALU.mult, ALU.mult), [("v_mx", 1)])
        wait("vector", "v_mx")
        wait("scalar", "v_mx")

    x2_spill_mark = {}
    l3_spill_mark = {}
    for ci in range(NCH):
        u_t = ub[ci % 2]
        if ci >= 2:
            wait("sync", "v_mx")
            wait("sync", "s_mx")
        dma("sync", u_t[:], u_d[ci * 128:(ci + 1) * 128, :], "d_u")
        wait("vector", "d_u", 16 * (ci + 1))
        wait("scalar", "d_u", 16 * (ci + 1))
        ln_stats(u_t, "ln1")
        if ci >= 1:
            wait("scalar", "t_tr")
        op("scalar", lambda e, u_t=u_t: e.activation(ln1b[:], u_t[:], AF.Identity,
                                                     bias=stat[:, 7:8], scale=stat[:, 6:7]),
           [("s_mx", 1)])
        if ci == 0:
            wait("tensor", "d_tx", tx_total)
            wait("tensor", "d_u", 16)
        wait("tensor", "s_mx")
        for jb in range(NJB):
            if not (ci == 0 and jb < 2):
                wait("tensor", "v_mx")
            op("tensor", lambda e, jb=jb: e.transpose(ps_tr[jb % 2][:],
                                                      ln1b[:, jb * 128:(jb + 1) * 128], ident[:]),
               [("t_tr", 1)])
            wait("vector", "t_tr")
            op("vector", lambda e, jb=jb: e.tensor_copy(ln1t[:, jb * 128:(jb + 1) * 128],
                                                        ps_tr[jb % 2][:]), [("v_mx", 1)])
        wait("tensor", "v_mx")
        for s_i in range(NSPAN):
            if s_i >= 4 or ci > 0:
                wait("tensor", "s_ev", cnt["s_ev"] - 3 if cnt["s_ev"] >= 4 else 0)
            jbs = list(range(0, min(NJB, 4 * s_i + 4)))
            for k, jb in enumerate(jbs):
                incs = [("t_cv", 1)] if k == len(jbs) - 1 else []
                op("tensor", lambda e, s_i=s_i, jb=jb, k=k, last=(k == len(jbs) - 1):
                   e.matmul(ps_cv[s_i % 4][:], ln1t[:, jb * 128:(jb + 1) * 128],
                            tallx[:, 512 + 512 * s_i - 128 * jb: 1024 + 512 * s_i - 128 * jb],
                            start=(k == 0), stop=last), incs)
            wait("scalar", "t_cv")
            op("scalar", lambda e, s_i=s_i: e.activation(ysb[:, s_i * 512:(s_i + 1) * 512],
                                                         ps_cv[s_i % 4][:], AF.Copy,
                                                         accum_out=stat[:, 8 + s_i:9 + s_i]),
               [("s_mx", 1)])
            op("scalar", lambda e, s_i=s_i: e.activation(junkC[:, s_i * 512:(s_i + 1) * 512],
                                                         ps_cv[s_i % 4][:], AF.Square,
                                                         accum_out=stat[:, 16 + s_i:17 + s_i]),
               [("s_ev", 1)])
        wait("vector", "s_mx")
        wait("vector", "s_ev")
        op("vector", lambda e: e.reduce_sum(stat[:, 0:1], stat[:, 8:16], AX.X), [("v_mx", 1)])
        op("vector", lambda e: e.reduce_sum(stat[:, 1:2], stat[:, 16:24], AX.X), [("v_mx", 1)])
        wait("vector", "v_mx")
        op("vector", lambda e: e.tensor_scalar_mul(stat[:, 2:3], stat[:, 0:1], 1.0 / D), [("v_mx", 1)])
        op("vector", lambda e: e.tensor_scalar_mul(stat[:, 3:4], stat[:, 1:2], 1.0 / D), [("v_mx", 1)])
        wait("vector", "v_mx")
        op("vector", lambda e: e.tensor_scalar(stat[:, 4:5], stat[:, 2:3], stat[:, 2:3], -1.0,
                                               ALU.mult, ALU.mult), [("v_mx", 1)])
        wait("vector", "v_mx")
        op("vector", lambda e: e.tensor_add(stat[:, 4:5], stat[:, 4:5], stat[:, 3:4]), [("v_mx", 1)])
        wait("vector", "v_mx")
        op("vector", lambda e: e.tensor_scalar_add(stat[:, 4:5], stat[:, 4:5], EPS_LN), [("v_mx", 1)])
        wait("scalar", "v_mx")
        op("scalar", lambda e: e.activation(stat[:, 5:6], stat[:, 4:5], AF.Sqrt), [("s_mx", 1)])
        wait("vector", "s_mx")
        op("vector", lambda e: e.reciprocal(stat[:, 6:7], stat[:, 5:6]), [("v_mx", 1)])
        wait("vector", "v_mx")
        op("vector", lambda e: e.tensor_scalar(stat[:, 7:8], stat[:, 2:3], stat[:, 6:7], -1.0,
                                               ALU.mult, ALU.mult), [("v_mx", 1)])
        wait("vector", "v_mx")
        if ci >= 1:
            wait("vector", "d_sp", x2_spill_mark[ci - 1])
        op("vector", lambda e: e.tensor_scalar(x2b[:], ysb[:], stat[:, 2:3], stat[:, 6:7],
                                               ALU.subtract, ALU.mult), [("v_mx", 1)])
        wait("vector", "v_mx")
        op("vector", lambda e, u_t=u_t: e.tensor_add(x2b[:], x2b[:], u_t[:]), [("v_mx", 1)])
        wait("vector", "v_mx")
        wait("scalar", "v_mx")
        dma("scalar", x2_d[ci], x2b[:], "d_sp")
        x2_spill_mark[ci] = cnt["d_sp"]
        ln_stats(x2b, "ln3")
        wait("scalar", "t_tr")
        op("scalar", lambda e: e.activation(ln3b[:], x2b[:], AF.Identity,
                                            bias=stat[:, 7:8], scale=stat[:, 6:7]), [("s_mx", 1)])
        wait("tensor", "s_mx")
        if ci >= 1:
            wait("vector", "d_sp", l3_spill_mark[ci - 1])
        for jb in range(NJB):
            wait("tensor", "v_mx")
            op("tensor", lambda e, jb=jb: e.transpose(ps_tr[jb % 2][:],
                                                      ln3b[:, jb * 128:(jb + 1) * 128], ident[:]),
               [("t_tr", 1)])
            wait("vector", "t_tr")
            op("vector", lambda e, jb=jb: e.tensor_copy(ln3t[:, jb * 128:(jb + 1) * 128],
                                                        ps_tr[jb % 2][:]), [("v_mx", 1)])
        blk_i, cc = divmod(ci, 4)
        wait("scalar", "v_mx")
        dma("scalar",
            bass.AP(l3t_d.tensor, (blk_i * NJB * 128 * TB) + cc * 128,
                    [[TB, 128], [128 * TB, NJB], [1, 128]]),
            ln3t.rearrange("p (j t) -> p j t", j=NJB),
            "d_sp")
        l3_spill_mark[ci] = cnt["d_sp"]
        emit_p1(18 if ci < 15 else 999)

    pmx2 = contextlib.ExitStack()
    p_sc = pmx2.enter_context(nc.psum_tensor("p_sc", [128, 12], F32))
    p_s1 = pmx2.enter_context(nc.psum_tensor("p_s1", [1, 4], F32))
    wait("vector", "s_mx")
    wait("vector", "v_mx")
    for j, (lo, hi) in enumerate([SLOT_G, SLOT_U, SLOT_D]):
        op("vector", lambda e, lo=lo, hi=hi, j=j: e.reduce_sum(
            stat[:, 24 + j:25 + j], p1slot[:, lo:hi], AX.X), [("v_mx", 1)])
    wait("tensor", "v_mx")
    for j in range(3):
        op("tensor", lambda e, j=j: e.matmul(p_s1[:, j:j + 1], stat[:, 24 + j:25 + j],
                                             ones128c[:], start=True, stop=True), [("t_bc", 1)])
    wait("vector", "t_bc")
    op("vector", lambda e: e.tensor_scalar_mul(svrow[:, 1:4], p_s1[:, 0:3], 1.0 / WG_N), [("v_mx", 1)])
    wait("vector", "v_mx")
    op("vector", lambda e: e.tensor_scalar(svrow[:, 7:10], svrow[:, 1:4], EPS_BL, 0.5,
                                           ALU.add, ALU.mult), [("v_mx", 1)])
    wait("vector", "v_mx")
    op("vector", lambda e: e.tensor_scalar_mul(svrow[:, 4:7], svrow[:, 7:10], -1.0), [("v_mx", 1)])
    op("vector", lambda e: e.tensor_mul(svrow[:, 10:11], svrow[:, 1:2], svrow[:, 2:3]), [("v_mx", 1)])
    wait("vector", "v_mx")
    op("vector", lambda e: e.tensor_scalar_mul(svrow[:, 10:11], svrow[:, 10:11], 0.25), [("v_mx", 1)])
    op("vector", lambda e: e.tensor_scalar_mul(svrow[:, 11:12], svrow[:, 3:4], 0.5), [("v_mx", 1)])
    wait("vector", "v_mx")
    op("vector", lambda e: e.tensor_mul(svrow[:, 12:13], svrow[:, 11:12], svrow[:, 11:12]), [("v_mx", 1)])
    wait("tensor", "v_mx")
    op("tensor", lambda e: e.matmul(p_sc[:], ones128[:], svrow[:, 1:13], start=True, stop=True),
       [("t_bc", 1)])
    wait("vector", "t_bc")
    op("vector", lambda e: e.tensor_copy(cscal[:], p_sc[:]), [("v_mx", 1)])
    NTHR = {"g": cscal[:, 3:4], "u": cscal[:, 4:5], "d": cscal[:, 5:6]}
    THR = {"g": cscal[:, 6:7], "u": cscal[:, 7:8], "d": cscal[:, 8:9]}
    mx_v_done = cnt["v_mx"]
    mx_s_done = cnt["s_mx"]
    mx_t_done = cnt["t_cv"]
    sp_done = cnt["d_sp"]

    pmx2.close()
    pmx.close()
    mx.close()

    ffn = contextlib.ExitStack()
    l3ts = ffn.enter_context(nc.sbuf_tensor("l3ts", [128, NJB * TB], BF16))
    inter = ffn.enter_context(nc.sbuf_tensor("inter", [128, NIB * TB], BF16))
    wgb = [ffn.enter_context(nc.sbuf_tensor(f"wgb{i}", [128, D], FP8)) for i in range(2)]
    wub = [ffn.enter_context(nc.sbuf_tensor(f"wub{i}", [128, D], FP8)) for i in range(2)]
    wdb = [ffn.enter_context(nc.sbuf_tensor(f"wdb{i}", [128, 2048], FP8)) for i in range(2)]
    wf32 = [ffn.enter_context(nc.sbuf_tensor(f"wf32{i}", [128, 1024], F32)) for i in range(2)]
    tqS = [ffn.enter_context(nc.sbuf_tensor(f"tqS{i}", [128, 1024], BF16)) for i in range(2)]
    tqV = [ffn.enter_context(nc.sbuf_tensor(f"tqV{i}", [128, 1024], BF16)) for i in range(2)]
    gsb = [ffn.enter_context(nc.sbuf_tensor(f"gsb{i}", [128, 512], BF16)) for i in range(2)]
    dstg = [ffn.enter_context(nc.sbuf_tensor(f"dstg{i}", [128, 512], F32)) for i in range(2)]
    apy = [ffn.enter_context(nc.sbuf_tensor(f"apy{i}", [128, 512], F32)) for i in range(2)]
    apx = [ffn.enter_context(nc.sbuf_tensor(f"apx{i}", [128, 512], F32)) for i in range(2)]
    fstat = ffn.enter_context(nc.sbuf_tensor("fstat", [128, 192], F32))
    junkF = ffn.enter_context(nc.sbuf_tensor("junkF", [128, 2048], BF16))

    pff = contextlib.ExitStack()
    ps_g = [pff.enter_context(nc.psum_tensor(f"ps_g{i}", [128, 512], F32)) for i in range(2)]
    ps_u = [pff.enter_context(nc.psum_tensor(f"ps_u{i}", [128, 512], F32)) for i in range(2)]
    ps_d = [pff.enter_context(nc.psum_tensor(f"ps_d{i}", [128, 512], F32)) for i in range(4)]
    ps8 = ps_d + ps_g + ps_u

    qs_state = {"last_add": 0}

    def q_s(dst, src, w, mat):
        wait("scalar", "p_q", qs_state["last_add"])
        op("scalar", lambda e: e.activation(tqS[0][:, 0:w], src, AF.Sign, bias=NTHR[mat]),
           [("s_q", 1)])
        op("scalar", lambda e: e.activation(tqS[1][:, 0:w], src, AF.Sign, bias=THR[mat]),
           [("s_q", 1)])
        wait("gpsimd", "s_q")
        op("gpsimd", lambda e: e.tensor_add(dst, tqS[0][:, 0:w], tqS[1][:, 0:w]), [("p_q", 1)])
        qs_state["last_add"] = cnt["p_q"]
        return ("p_q", cnt["p_q"]), ("s_q", cnt["s_q"])

    def q_v(dst, src, w, mat):
        wait("vector", "v_q")
        op("vector", lambda e: e.tensor_scalar(tqV[0][:, 0:w], src, THR[mat], 2.0,
                                               ALU.is_ge, ALU.mult), [("v_q", 1)])
        op("vector", lambda e: e.tensor_scalar(tqV[1][:, 0:w], src, NTHR[mat], 2.0,
                                               ALU.is_le, ALU.mult), [("v_q", 1)])
        wait("vector", "v_q")
        op("vector", lambda e: e.tensor_sub(dst, tqV[0][:, 0:w], tqV[1][:, 0:w]), [("v_q", 1)])
        return ("v_q", cnt["v_q"]), ("v_q", cnt["v_q"])


    wf32_mark = [None, None]

    def quant_1024(dst, src_dram_ap, mat, path):
        bi = quant_1024.tick % 2
        q, dsem = (("sync", "d_w") if bi == 0 else ("scalar", "d_w2"))
        quant_1024.tick += 1
        if wf32_mark[bi] is not None:
            wait(q, *wf32_mark[bi])
        dma(q, wf32[bi][:], src_dram_ap, dsem)
        eng = {"s": "scalar", "v": "vector", "p": "gpsimd"}[path]
        wait(eng, dsem, cnt[dsem])
        m = {"s": q_s, "v": q_v}[path](dst, wf32[bi][:], 1024, mat)
        wf32_mark[bi] = m[1]
        return m[0]
    quant_1024.tick = 0

    wb_count = {}
    guse = {}
    dstg_mark = {}
    ap_mark = {}

    for blk_i in range(NBLK):
        jit = (blk_i == 0)
        fb = (blk_i % 2) * 96
        if blk_i == 0:
            wait("sync", "d_sp", sp_done)
            wait("sync", "v_mx", mx_v_done)
            wait("sync", "s_mx", mx_s_done)
            wait("sync", "t_cv", mx_t_done)
            wait("vector", "v_mx", mx_v_done)
            wait("scalar", "v_mx", mx_v_done)
            wait("gpsimd", "v_mx", mx_v_done)
        else:
            wait("sync", "t_g", 172 * blk_i)
        dma("sync", l3ts.rearrange("p (j t) -> p j t", j=NJB),
            l3t_d[blk_i].rearrange("j p t -> p j t"), "d_l3")
        l3_target = cnt["d_l3"]

        for i in range(NIB):
            par = i % 2
            if i == 0:
                wait("vector", "t_di", 344 * blk_i)
            qm = {}
            if jit:
                for mat, wl, wq in (("g", wgb, wgq_d), ("u", wub, wuq_d)):
                    if ("wgbuf", mat, par) in guse:
                        wait("vector", "t_g", guse[("wgbuf", mat, par)])
                    if ("wgwb", mat, par) in guse:
                        wait("vector", "d_wb", guse[("wgwb", mat, par)])
                    paths = ("s", "v", "s", "v") if mat == "g" else ("v", "s", "v", "s")
                    for qq in range(4):
                        src = bass.AP((wgT if mat == "g" else wuT).tensor,
                                      i * 128 + qq * 8 * 128 * I_DIM,
                                      [[I_DIM, 128], [128 * I_DIM, 8], [1, 128]])
                        m = quant_1024(wl[par][:, qq * 1024:(qq + 1) * 1024],
                                       src, mat, paths[qq])
                        qm[(mat, qq)] = m
                    need = {}
                    for qq in range(4):
                        sname, sval = qm[(mat, qq)]
                        need[sname] = max(need.get(sname, 0), sval)
                    for sname, sval in need.items():
                        wait("gpsimd", sname, sval)
                    dma("gpsimd", wq[i].rearrange("a p f -> p a f"),
                        wl[par][:].rearrange("p (a f) -> p a f", a=32), "d_wb")
                    wb_count[(mat, i)] = cnt["d_wb"]
                    guse[("wgwb", mat, par)] = cnt["d_wb"]
            else:
                for mat, wl, wq, rq, rsem in (("g", wgb, wgq_d, "gpsimd", "d_wr"),
                                              ("u", wub, wuq_d, "sync", "d_wrv")):
                    if ("wgbuf", mat, par) in guse:
                        wait(rq, "t_g", guse[("wgbuf", mat, par)])
                    if ("wgwb", mat, par) in guse:
                        wait(rq, "d_wb", guse[("wgwb", mat, par)])
                        del guse[("wgwb", mat, par)]
                    wait(rq, "d_wb", wb_count[(mat, i)])
                    dma(rq, wl[par][:].rearrange("p (a f) -> p a f", a=32),
                        wq[i].rearrange("a p f -> p a f"), rsem)
            if i == 0:
                wait("tensor", "d_l3", l3_target)
                if blk_i == 0:
                    wait("tensor", "v_mx", mx_v_done)
                    wait("tensor", "s_mx", mx_s_done)
                else:
                    wait("tensor", "s_ev")
            if jit:
                wait("tensor", "v_q")
                wait("tensor", "s_q")
                wait("tensor", "p_q")
            else:
                wait("tensor", "d_wr")
                wait("tensor", "d_wrv")
            if ("psg", par) in guse:
                wait("tensor", "s_ev", guse[("psg", par)])
            if ("psu", par) in guse:
                wait("tensor", "v_ff", guse[("psu", par)])
            for psl, wl, mat in ((ps_g, wgb, "g"), (ps_u, wub, "u")):
                for ht in range(32):
                    incs = [("t_g", 1)] if ht == 31 else []
                    op("tensor", lambda e, psl=psl, wl=wl, par=par, ht=ht, last=(ht == 31):
                       e.matmul(psl[par][:], wl[par][:, ht * 128:(ht + 1) * 128],
                                l3ts[:, ht * TB:(ht + 1) * TB],
                                start=(ht == 0), stop=last), incs)
                guse[("wgbuf", mat, par)] = cnt["t_g"]
            wait("scalar", "t_g", cnt["t_g"] - 1)
            if ("gsb", par) in guse:
                wait("scalar", "v_ff", guse[("gsb", par)])
            op("scalar", lambda e, par=par: e.activation(gsb[par][:], ps_g[par][:], AF.Copy,
                                                         scale=cscal[:, 9:10]), [("s_ev", 1)])
            guse[("psg", par)] = cnt["s_ev"]
            wait("vector", "t_g")
            wait("vector", "s_ev")
            op("vector", lambda e, par=par, i=i: e.tensor_mul(
                inter[:, i * TB:(i + 1) * TB], gsb[par][:], ps_u[par][:]), [("v_ff", 1)])
            guse[("psu", par)] = cnt["v_ff"]
            guse[("gsb", par)] = cnt["v_ff"]

        v_ff_all = cnt["v_ff"]
        for tsp in range(2):
          for sppP in range(2):
            for i in range(NIB):
                par = i % 2
                qjit = jit and tsp == 0
                if ("wdb", par) in guse:
                    gk = guse[("wdb", par)]
                    if qjit:
                        wait("vector", "t_di", gk)
                        wait("scalar", "t_di", gk)
                        wait("gpsimd", "t_di", gk)
                        if ("wdwb", par) in guse:
                            wait("vector", "d_wb", guse[("wdwb", par)])
                            wait("scalar", "d_wb", guse[("wdwb", par)])
                            wait("gpsimd", "d_wb", guse[("wdwb", par)])
                    else:
                        wait("scalar", "t_di", gk)
                if qjit:
                    mqs = []
                    for half in range(2):
                        srcap = bass.AP(wdT.tensor, i * 128 * D + sppP * 2048 + half * 1024,
                                        [[D, 128], [1, 1024]])
                        mqs.append(quant_1024(wdb[par][:, half * 1024:(half + 1) * 1024],
                                              srcap, "d", ("s", "v")[(2 * i + half) % 2]))
                    need = {}
                    for sname, sval in mqs:
                        need[sname] = max(need.get(sname, 0), sval)
                    for sname, sval in need.items():
                        wait("gpsimd", sname, sval)
                    dma("gpsimd", wdq_d[sppP, i], wdb[par][:], "d_wb")
                    wb_count[("d", sppP, i)] = cnt["d_wb"]
                    guse[("wdwb", par)] = cnt["d_wb"]
                else:
                    if ("wdwb", par) in guse:
                        wait("scalar", "d_wb", guse[("wdwb", par)])
                        del guse[("wdwb", par)]
                    wait("scalar", "d_wb", wb_count[("d", sppP, i)])
                    dma("scalar", wdb[par][:], wdq_d[sppP, i], "d_wds")
                if i == 0:
                    if tsp == 0 and sppP == 0:
                        wait("tensor", "v_ff", v_ff_all)
                    wait("tensor", "s_ev")
                if qjit:
                    wait("tensor", "v_q")
                    wait("tensor", "s_q")
                    wait("tensor", "p_q")
                else:
                    wait("tensor", "d_wds")
                for tsl in range(2):
                    ts_i = 2 * tsp + tsl
                    for c in range(4):
                        incs = [("t_di", 1)] if (tsl == 1 and c == 3) else []
                        op("tensor", lambda e, par=par, i=i, ts_i=ts_i, tsl=tsl, c=c,
                           last=(i == NIB - 1):
                           e.matmul(ps8[tsl * 4 + c][:],
                                    inter[:, i * TB + ts_i * 128: i * TB + (ts_i + 1) * 128],
                                    wdb[par][:, c * 512:(c + 1) * 512],
                                    start=(i == 0), stop=last), incs)
                guse[("wdb", par)] = cnt["t_di"]
            wait("scalar", "t_di", cnt["t_di"])
            if blk_i >= 2 and tsp == 0 and sppP == 0:
                wait("scalar", "v_st", guse[("astat", blk_i - 2)])
            for idx in range(8):
                tsl, c = divmod(idx, 4)
                ts_i = 2 * tsp + tsl
                span = 4 * sppP + c
                if ("dstg", idx % 2) in dstg_mark:
                    wait("scalar", "d_sp", dstg_mark[("dstg", idx % 2)])
                op("scalar", lambda e, idx=idx, ts_i=ts_i, span=span, fb=fb: e.activation(
                    dstg[idx % 2][:], ps8[idx][:], AF.Copy, scale=cscal[:, 10:11],
                    accum_out=fstat[:, fb + ts_i * 8 + span: fb + ts_i * 8 + span + 1]),
                   [("s_ev", 1)])
                wait("scalar", "s_ev")
                op("scalar", lambda e, idx=idx, ts_i=ts_i, span=span, fb=fb: e.activation(
                    junkF[:, (idx % 4) * 512:(idx % 4 + 1) * 512], ps8[idx][:], AF.Square,
                    accum_out=fstat[:, fb + 32 + ts_i * 8 + span: fb + 33 + ts_i * 8 + span]),
                   [("s_ev", 1)])
                wait("scalar", "s_ev")
                dma("scalar", dsb_d[blk_i, ts_i, :, span * 512:(span + 1) * 512],
                    dstg[idx % 2][:], "d_sp")
                dstg_mark[("dstg", idx % 2)] = cnt["d_sp"]
        dsb_done = cnt["d_sp"]
        d_ev_done = cnt["s_ev"]

        for ts_i in range(NTSUB):
            ci = blk_i * 4 + ts_i
            wait("vector", "s_ev", d_ev_done)
            op("vector", lambda e, ts_i=ts_i, fb=fb: e.reduce_sum(
                fstat[:, fb + 64 + ts_i:fb + 65 + ts_i],
                fstat[:, fb + ts_i * 8:fb + (ts_i + 1) * 8], AX.X), [("v_st", 1)])
            op("vector", lambda e, ts_i=ts_i, fb=fb: e.reduce_sum(
                fstat[:, fb + 68 + ts_i:fb + 69 + ts_i],
                fstat[:, fb + 32 + ts_i * 8:fb + 32 + (ts_i + 1) * 8], AX.X), [("v_st", 1)])
            wait("vector", "v_st")
            op("vector", lambda e, ts_i=ts_i, fb=fb: e.tensor_scalar_mul(
                fstat[:, fb + 72 + ts_i:fb + 73 + ts_i],
                fstat[:, fb + 64 + ts_i:fb + 65 + ts_i], 1.0 / D), [("v_st", 1)])
            op("vector", lambda e, ts_i=ts_i, fb=fb: e.tensor_scalar(
                fstat[:, fb + 76 + ts_i:fb + 77 + ts_i],
                fstat[:, fb + 68 + ts_i:fb + 69 + ts_i],
                cscal[:, 11:12], 1.0 / D, ALU.mult, ALU.mult), [("v_st", 1)])
            wait("vector", "v_st")
            op("vector", lambda e, ts_i=ts_i, fb=fb: e.tensor_scalar(
                fstat[:, fb + 80 + ts_i:fb + 81 + ts_i],
                fstat[:, fb + 72 + ts_i:fb + 73 + ts_i],
                fstat[:, fb + 72 + ts_i:fb + 73 + ts_i], -1.0, ALU.mult, ALU.mult), [("v_st", 1)])
            wait("vector", "v_st")
            op("vector", lambda e, ts_i=ts_i, fb=fb: e.tensor_add(
                fstat[:, fb + 80 + ts_i:fb + 81 + ts_i],
                fstat[:, fb + 80 + ts_i:fb + 81 + ts_i],
                fstat[:, fb + 76 + ts_i:fb + 77 + ts_i]), [("v_st", 1)])
            wait("vector", "v_st")
            op("vector", lambda e, ts_i=ts_i, fb=fb: e.tensor_scalar_add(
                fstat[:, fb + 80 + ts_i:fb + 81 + ts_i],
                fstat[:, fb + 80 + ts_i:fb + 81 + ts_i], EPS_LN), [("v_st", 1)])
            wait("scalar", "v_st")
            op("scalar", lambda e, ts_i=ts_i, fb=fb: e.activation(
                fstat[:, fb + 84 + ts_i:fb + 85 + ts_i],
                fstat[:, fb + 80 + ts_i:fb + 81 + ts_i], AF.Sqrt), [("s_ff", 1)])
            wait("vector", "s_ff")
            op("vector", lambda e, ts_i=ts_i, fb=fb: e.reciprocal(
                fstat[:, fb + 88 + ts_i:fb + 89 + ts_i],
                fstat[:, fb + 84 + ts_i:fb + 85 + ts_i]), [("v_st", 1)])
            wait("vector", "v_st")
            if ts_i == NTSUB - 1:
                guse[("astat", blk_i)] = cnt["v_st"]
            for hh in range(8):
                bi2 = hh % 2
                if ("ap", bi2) in ap_mark:
                    wait("sync", "v_ap", ap_mark[("ap", bi2)][0])
                    wait("sync", "d_out", ap_mark[("ap", bi2)][1])
                wait("sync", "d_sp", dsb_done)
                dma("sync", apy[bi2][:], dsb_d[blk_i, ts_i, :, hh * 512:(hh + 1) * 512], "d_ap")
                dma("sync", apx[bi2][:], x2_d[ci, :, hh * 512:(hh + 1) * 512], "d_ap")
                wait("vector", "d_ap")
                op("vector", lambda e, bi2=bi2, ts_i=ts_i, fb=fb: e.tensor_scalar(
                    apy[bi2][:], apy[bi2][:], fstat[:, fb + 72 + ts_i:fb + 73 + ts_i],
                    fstat[:, fb + 88 + ts_i:fb + 89 + ts_i], ALU.subtract, ALU.mult), [("v_ap", 1)])
                wait("vector", "v_ap")
                op("vector", lambda e, bi2=bi2: e.tensor_add(apy[bi2][:], apy[bi2][:], apx[bi2][:]),
                   [("v_ap", 1)])
                wait("scalar", "v_ap")
                dma("scalar", out_d[ci * 128:(ci + 1) * 128, hh * 512:(hh + 1) * 512],
                    apy[bi2][:], "d_out")
                ap_mark[("ap", bi2)] = (cnt["v_ap"], cnt["d_out"])

    wait("sync", "d_out", cnt["d_out"])
    wait("gpsimd", "d_wb", cnt["d_wb"])

    with nc.Block() as blk:
        @blk.sync
        def _(e):
            for fn in ops["sync"]:
                fn(e)

        @blk.tensor
        def _(e):
            for fn in ops["tensor"]:
                fn(e)

        @blk.vector
        def _(e):
            for fn in ops["vector"]:
                fn(e)

        @blk.scalar
        def _(e):
            for fn in ops["scalar"]:
                fn(e)

        @blk.gpsimd
        def _(e):
            for fn in ops["gpsimd"]:
                fn(e)

    pff.close()
    ffn.close()
    glob.close()
    st.close()
    return nc


def kernel(L0_attn_u, L0_attn_Lambda_ri, L0_attn_p_ri, L0_attn_q_ri, L0_attn_Vc_ri,
           L0_attn_Ct_ri, L0_attn_Bp, L0_attn_D, L0_attn_log_step,
           L0_ffn_w_gate, L0_ffn_w_up, L0_ffn_w_down):
    u = np.ascontiguousarray(np.asarray(L0_attn_u, dtype=np.float32))
    wgT = np.ascontiguousarray(np.asarray(L0_ffn_w_gate, dtype=np.float32).T)
    wuT = np.ascontiguousarray(np.asarray(L0_ffn_w_up, dtype=np.float32).T)
    wdT = np.ascontiguousarray(np.asarray(L0_ffn_w_down, dtype=np.float32).T)
    lam = np.asarray(L0_attn_Lambda_ri, np.float32)
    p_ri = np.asarray(L0_attn_p_ri, np.float32)
    q_ri = np.asarray(L0_attn_q_ri, np.float32)
    vc = np.asarray(L0_attn_Vc_ri, np.float32)
    ct = np.asarray(L0_attn_Ct_ri, np.float32).reshape(64, 2)
    bp = np.asarray(L0_attn_Bp, np.float32).reshape(64)
    scol = np.stack([lam[:, 0], lam[:, 1], p_ri[:, 0], p_ri[:, 1],
                     q_ri[:, 0], q_ri[:, 1], ct[:, 0], ct[:, 1]], axis=1)
    scol = np.ascontiguousarray(scol, np.float32)
    dls = np.array([[float(np.asarray(L0_attn_D).reshape(-1)[0]),
                     float(np.asarray(L0_attn_log_step).reshape(-1)[0])]], np.float32)
    tabs = _host_tables()

    nc = build_program()
    base = {
        "wgT": wgT, "wuT": wuT, "wdT": wdT,
        "scol": scol, "vcr": np.ascontiguousarray(vc[:, :, 0]),
        "vci": np.ascontiguousarray(vc[:, :, 1]),
        "bprow": bp[None, :], "dls": dls, "ident": tabs["ident"],
        "cvals": np.array([[-1.0, math.log(2.0), 0.0, 0.0]], np.float32),
    }
    for nm in ["t1r", "t1i", "t2r", "t2i", "wa", "wa2", "twr", "twi", "wbc", "wbs",
               "ones1x64", "ones1x128", "ones64x1", "ones128x1"]:
        base[nm] = tabs[nm]
    in_maps = [dict(base, u=np.ascontiguousarray(u[c])) for c in range(8)]
    res = run_bass_kernel_spmd(nc, in_maps, core_ids=list(range(8)))
    out = np.stack([res.results[c]["out"] for c in range(8)], axis=0)
    return out
